# version 01
# baseline (speedup 1.0000x reference)
from contextlib import ExitStack
import numpy as np
import ml_dtypes
import concourse.bass as bass
import concourse.mybir as mybir
from concourse.bass_utils import run_bass_kernel_spmd

F32 = mybir.dt.float32
BF16 = mybir.dt.bfloat16
AF = mybir.ActivationFunctionType
ALU = mybir.AluOpType
AX = mybir.AxisListType

NCORES = 8
D = 2048
T = 2048
NK = D // 128
TB = 512
NTB = T // TB

BF = ml_dtypes.bfloat16
RSQRT8 = float(1.0 / np.sqrt(8.0))
DEBUG = False
L1B_STAGE = 3
L1B_PACKBAR = False
L1B_DEBUG = False
L0B_DEBUG = False


class Sem:

    def __init__(self, nc, name):
        self.h = nc.alloc_semaphore(name)
        self.n = 0

    def inc(self, instr, k=1):
        instr.then_inc(self.h, k)
        self.n += k
        return instr

    def dma(self, instr):
        return self.inc(instr, 16)

    def wait(self, eng):
        eng.wait_ge(self.h, self.n)


class Ctx:
    def __init__(self, nc, block, dummy=None):
        self.nc = nc
        self.block = block
        self.sems = {}
        self._cpool = None
        self._cpool_vals = []
        self._dummy = dummy

    def init_cpool(self, tile, values):
        self._cpool = tile
        self._cpool_vals = list(values)
        blk = self.block

        @blk.vector
        def _(v):
            for i, val in enumerate(self._cpool_vals):
                v.memset(tile[:, i:i + 1], float(val))
        for i, val in enumerate(self._cpool_vals):
            self.nc.const_aps.register(F32, float(val), tile[:, i:i + 1])

    def sem(self, name):
        if name not in self.sems:
            self.sems[name] = Sem(self.nc, name)
        return self.sems[name]

    def slot_sems(self, name, n):
        return [self.sem(f"{name}{i}") for i in range(n)]

    def barrier(self, engines=("tensor", "vector", "scalar", "sync", "gpsimd")):
        bar = self.sem("bar")
        blk = self.block
        dmy = self._dummy
        for e in engines:
            def mk(eng, _e=e):
                if _e == "vector":
                    bar.inc(eng.memset(dmy[:, 0:1], 0.0), 1)
                elif _e == "scalar":
                    bar.inc(eng.mul(dmy[:, 1:2], dmy[:, 1:2], 1.0), 1)
                elif _e == "gpsimd":
                    bar.inc(eng.memset(dmy[:, 2:3], 0.0), 1)
                else:
                    bar.inc(eng.nop(), 1)
            getattr(blk, e)(mk)
        target = bar.n
        for e in engines:
            def mk2(eng):
                eng.wait_ge(bar.h, target)
            getattr(blk, e)(mk2)


def consts_for_core():
    return {
        "c_ones_bf_1x128": np.ones((1, 128), dtype=BF),
        "c_ones_bf_1x512": np.ones((1, 512), dtype=BF),
        "c_s8_bf_1x512": np.full((1, 512), np.sqrt(8.0), dtype=BF),
        "c_ones_f_128x1": np.ones((128, 1), dtype=np.float32),
        "c_ones_f_1x128": np.ones((1, 128), dtype=np.float32),
        "c_ident_bf": np.eye(128, dtype=BF),
    }


_CONST_SPECS = [
    ("c_ones_bf_1x128", (1, 128), BF16),
    ("c_ones_bf_1x512", (1, 512), BF16),
    ("c_s8_bf_1x512", (1, 512), BF16),
    ("c_ones_f_128x1", (128, 1), F32),
    ("c_ones_f_1x128", (1, 128), F32),
    ("c_ident_bf", (128, 128), BF16),
]


def declare_consts(nc):
    return {name: nc.dram_tensor(name, list(shape), dt, kind="ExternalInput").ap()
            for name, shape, dt in _CONST_SPECS}



def emit_ln_rows(ctx, res, xt, sq, rows, rows_b, ones_f, psA, psB):
    nc, block = ctx.nc, ctx.block
    ld, mm, ev, ac = ctx.sem("ld"), ctx.sem("mm"), ctx.sem("ev"), ctx.sem("ac")

    NTILE = NTB * NK
    mm0, ev0, ac0 = mm.n, ev.n, ac.n
    lds = ctx.slot_sems("lnld", 4)
    lds0 = [s.n for s in lds]

    @block.sync
    def _(sy):
        for i in range(NTILE):
            tb, k = divmod(i, NK)
            if i >= 4:
                sy.wait_ge(mm.h, mm0 + 2 * (i - 4) + 2)
                sy.wait_ge(ac.h, ac0 + (i - 4) + 1)
            sy.dma_start(out=xt[:, (i % 4) * TB:(i % 4 + 1) * TB],
                         in_=res[k * 128:(k + 1) * 128, tb * TB:(tb + 1) * TB]
                         ).then_inc(lds[i % 4].h, 16)
            lds[i % 4].n += 16

    @block.scalar
    def _(s):
        for i in range(NTILE):
            s.wait_ge(lds[i % 4].h, lds0[i % 4] + 16 * (i // 4 + 1))
            if i >= 2:
                s.wait_ge(mm.h, mm0 + 2 * (i - 2) + 2)
            ac.inc(s.activation(sq[:, (i % 2) * TB:(i % 2 + 1) * TB],
                                xt[:, (i % 4) * TB:(i % 4 + 1) * TB], AF.Square))

    @block.tensor
    def _(t):
        for i in range(NTILE):
            tb, k = divmod(i, NK)
            if k == 0 and tb > 0:
                t.wait_ge(ev.h, ev0 + 2 * tb)
            t.wait_ge(lds[i % 4].h, lds0[i % 4] + 16 * (i // 4 + 1))
            mm.inc(t.matmul(psA[:, :], lhsT=ones_f[:], rhs=xt[:, (i % 4) * TB:(i % 4 + 1) * TB],
                            start=(k == 0), stop=(k == NK - 1), skip_group_check=True))
            t.wait_ge(ac.h, ac0 + i + 1)
            mm.inc(t.matmul(psB[:, :], lhsT=ones_f[:], rhs=sq[:, (i % 2) * TB:(i % 2 + 1) * TB],
                            start=(k == 0), stop=(k == NK - 1), skip_group_check=True))

    @block.vector
    def _(v):
        for tb in range(NTB):
            v.wait_ge(mm.h, mm0 + 2 * ((tb + 1) * NK - 1) + 2)
            ev.inc(v.tensor_copy(rows[:, 0 * D + tb * TB: 0 * D + (tb + 1) * TB], psA[:, :]))
            ev.inc(v.tensor_copy(rows[:, 1 * D + tb * TB: 1 * D + (tb + 1) * TB], psB[:, :]))

    ctx.barrier()

    MM_, VAR, RSTD, MR = 2, 3, 4, 5

    @block.scalar
    def _(s):
        ac.inc(s.activation(rows[:, MM_ * D:(MM_ + 1) * D], rows[:, 0 * D:1 * D], AF.Copy, scale=1.0 / D))
    a1 = ac.n

    @block.vector
    def _(v):
        v.wait_ge(ac.h, a1)
        v.tensor_mul(rows[:, VAR * D:(VAR + 1) * D], rows[:, MM_ * D:(MM_ + 1) * D],
                     rows[:, MM_ * D:(MM_ + 1) * D])
        ev.inc(v.scalar_tensor_tensor(rows[:, VAR * D:(VAR + 1) * D], rows[:, 1 * D:2 * D], 1.0 / D,
                                      rows[:, VAR * D:(VAR + 1) * D], ALU.mult, ALU.subtract))
    e1 = ev.n

    @block.scalar
    def _(s):
        s.wait_ge(ev.h, e1)
        s.activation(rows[:, RSTD * D:(RSTD + 1) * D], rows[:, VAR * D:(VAR + 1) * D], AF.Ln, bias=1e-5)
        ac.inc(s.activation(rows[:, RSTD * D:(RSTD + 1) * D], rows[:, RSTD * D:(RSTD + 1) * D], AF.Exp, scale=-0.5))
    a2 = ac.n

    @block.vector
    def _(v):
        v.wait_ge(ac.h, a2)
        v.tensor_mul(rows[:, MR * D:(MR + 1) * D], rows[:, MM_ * D:(MM_ + 1) * D],
                     rows[:, RSTD * D:(RSTD + 1) * D])
        v.tensor_copy(rows_b[:, 0:D], rows[:, RSTD * D:(RSTD + 1) * D])
        ev.inc(v.tensor_copy(rows_b[:, D:2 * D], rows[:, MR * D:(MR + 1) * D]))

    ctx.barrier()


def build_l0ffn():
    nc = bass.Bass("TRN2", target_bir_lowering=False)
    res = nc.dram_tensor("res", [D, T], F32, kind="ExternalInput").ap()
    factors = nc.dram_tensor("factors", [8, D, D], F32, kind="ExternalInput").ap()
    bias_row = nc.dram_tensor("bias_row", [1, D], F32, kind="ExternalInput").ap()
    res_out = nc.dram_tensor("res_out", [D, T], F32, kind="ExternalOutput").ap()
    if DEBUG:
        dbg_rows = nc.dram_tensor("dbg_rows", [1, 6 * D], F32, kind="ExternalOutput").ap()
        dbg_F = nc.dram_tensor("dbg_F", [128, NK * D], F32, kind="ExternalOutput").ap()
        dbg_xh = nc.dram_tensor("dbg_xh", [128, NK * TB], F32, kind="ExternalOutput").ap()
    cst = declare_consts(nc)

    with ExitStack() as stack:
        sb = lambda *a, **k: stack.enter_context(nc.sbuf_tensor(*a, **k))
        ps = lambda *a, **k: stack.enter_context(nc.psum_tensor(*a, **k))
        Fb = sb("Fb", [128, NK * D], BF16)
        ftile = sb("ftile", [128, 16 * 512], BF16)
        xt = sb("xt", [128, NK * TB], F32)
        sq = sb("sq", [128, 2 * TB], F32)
        xh = sb("xh", [128, NK * TB], BF16)
        rows = sb("rows", [1, 6 * D], F32)
        rows_b = sb("rows_b", [1, 2 * D], BF16)
        bias_sb = sb("bias_sb", [1, D], BF16)
        ones_bf = sb("ones_bf", [1, 128], BF16)
        ones_bf512 = sb("ones_bf512", [1, 512], BF16)
        s8row = sb("s8row", [1, 512], BF16)
        ones_f = sb("ones_f", [128, 1], F32)
        identF = sb("identF", [128, 128], BF16)
        outt = sb("outt", [128, 4 * TB], F32)
        cpool = sb("cpool", [128, 4], F32)
        psF = ps("psF", [128, 512], F32)
        psA = ps("psA", [1, 512], F32)
        psB = ps("psB", [1, 512], F32)
        psRep = ps("psRep", [128, 2 * 512], F32)
        psO = ps("psO", [128, 2 * 512], F32)
        dmy = sb("dmy", [1, 4], F32)
        block = stack.enter_context(nc.Block())
        ctx = Ctx(nc, block, dummy=dmy)
        ld, ldg, mm, ev, ac, st = (ctx.sem(x) for x in ("ld", "ldg", "mm", "ev", "ac", "st"))
        ctx.init_cpool(cpool, [1e-5])
        ctx.barrier(("vector", "scalar", "tensor"))

        @block.sync
        def _(sy):
            ld.dma(sy.dma_start(out=ones_bf[:], in_=cst["c_ones_bf_1x128"][:, :]))
            ld.dma(sy.dma_start(out=ones_bf512[:], in_=cst["c_ones_bf_1x512"][:, :]))
            ld.dma(sy.dma_start(out=s8row[:], in_=cst["c_s8_bf_1x512"][:, :]))
            ld.dma(sy.dma_start(out=ones_f[:], in_=cst["c_ones_f_128x1"][:, :]))
            ld.dma(sy.dma_start(out=identF[:], in_=cst["c_ident_bf"][:, :]))
        c_ld = ld.n

        @block.gpsimd
        def _(g):
            ldg.dma(g.dma_start(out=bias_sb[:], in_=bias_row[:, :]))

        mm0, ac0 = mm.n, ac.n
        NREG = NK * 4
        fsl = ctx.slot_sems("fsl", 2)

        @block.gpsimd
        def _(g):
            for i in range(NREG):
                k, ob = divmod(i, 4)
                if i >= 2:
                    g.wait_ge(ac.h, ac0 + (i - 2) + 1)
                for r in range(8):
                    g.dma_start(out=ftile[:, ((i % 2) * 8 + r) * 512:((i % 2) * 8 + r + 1) * 512],
                                in_=factors[r, k * 128:(k + 1) * 128, ob * 512:(ob + 1) * 512]
                                ).then_inc(fsl[i % 2].h, 16)
                    fsl[i % 2].n += 16

        @block.tensor
        def _(t):
            t.wait_ge(ld.h, c_ld)
            for i in range(NREG):
                if i >= 1:
                    t.wait_ge(ac.h, ac0 + i)
                t.wait_ge(fsl[i % 2].h, 8 * 16 * (i // 2 + 1))
                for r in range(8):
                    mm.inc(t.matmul(psF[:, :], lhsT=identF[:],
                                    rhs=ftile[:, ((i % 2) * 8 + r) * 512:((i % 2) * 8 + r + 1) * 512],
                                    start=(r == 0), stop=(r == 7), skip_group_check=True))

        @block.scalar
        def _(s):
            for i in range(NREG):
                k, ob = divmod(i, 4)
                s.wait_ge(mm.h, mm0 + 8 * (i + 1))
                ac.inc(s.copy(Fb[:, k * D + ob * 512: k * D + (ob + 1) * 512], psF[:, :]))

        ctx.barrier()

        emit_ln_rows(ctx, res, xt, sq, rows, rows_b, ones_f, psA, psB)

        stS = ctx.slot_sems("stS", 4)
        for tb in range(NTB):
            @block.sync
            def _(sy, tb=tb):
                for k in range(NK):
                    sy.dma_start(out=xt[:, k * TB:(k + 1) * TB],
                                 in_=res[k * 128:(k + 1) * 128, tb * TB:(tb + 1) * TB]).then_inc(ld.h, 16)
            ld.n += NK * 16

            mm1 = mm.n

            @block.tensor
            def _(t, tb=tb):
                mm.inc(t.matmul(psRep[:, 0:512], lhsT=ones_bf[:],
                                rhs=rows_b[:, 0 * D + tb * TB: 0 * D + (tb + 1) * TB],
                                start=True, stop=True))
                mm.inc(t.matmul(psRep[:, 512:1024], lhsT=ones_bf[:],
                                rhs=rows_b[:, 1 * D + tb * TB: 1 * D + (tb + 1) * TB],
                                start=True, stop=True))

            ev1 = ev.n

            @block.vector
            def _(v, tb=tb, mm1=mm1):
                v.wait_ge(mm.h, mm1 + 2)
                ld.wait(v)
                for k in range(NK):
                    sl = (k % 2) * TB
                    v.tensor_tensor(sq[:, sl:sl + TB], xt[:, k * TB:(k + 1) * TB], psRep[:, 0:512], ALU.mult)
                    ev.inc(v.tensor_tensor(xh[:, k * TB:(k + 1) * TB], sq[:, sl:sl + TB],
                                           psRep[:, 512:1024], ALU.subtract))

            mm2 = mm.n
            evo = ev.n

            @block.tensor
            def _(t, tb=tb, ev1=ev1, evo=evo):
                ldg.wait(t)
                for oc in range(NK):
                    pb = oc % 2
                    if oc >= 2:
                        t.wait_ge(ev.h, evo + (oc - 2) + 1)
                    for k in range(NK):
                        t.wait_ge(ev.h, ev1 + k + 1)
                        mm.inc(t.matmul(psO[:, pb * 512:(pb + 1) * 512],
                                        lhsT=Fb[:, k * D + oc * 128: k * D + oc * 128 + 128],
                                        rhs=xh[:, k * TB:(k + 1) * TB],
                                        start=(k == 0), stop=False, skip_group_check=True))
                    mm.inc(t.matmul(psO[:, pb * 512:(pb + 1) * 512],
                                    lhsT=bias_sb[:, oc * 128:(oc + 1) * 128],
                                    rhs=s8row[:, :], start=False, stop=True,
                                    skip_group_check=True))

            stS0 = [s.n for s in stS]

            @block.vector
            def _(v, tb=tb, mm2=mm2, stS0=stS0):
                for oc in range(NK):
                    pb = oc % 2
                    v.wait_ge(mm.h, mm2 + (oc + 1) * (NK + 1))
                    if oc >= 4:
                        v.wait_ge(stS[oc % 4].h, stS0[oc % 4] + 16 * (oc // 4))
                    ev.inc(v.scalar_tensor_tensor(outt[:, (oc % 4) * TB:(oc % 4 + 1) * TB],
                                                  psO[:, pb * 512:(pb + 1) * 512], RSQRT8,
                                                  xt[:, oc * TB:(oc + 1) * TB],
                                                  ALU.mult, ALU.add))

            @block.sync
            def _(sy, tb=tb, evo=evo):
                for oc in range(NK):
                    sy.wait_ge(ev.h, evo + oc + 1)
                    sy.dma_start(out=res_out[oc * 128:(oc + 1) * 128, tb * TB:(tb + 1) * TB],
                                 in_=outt[:, (oc % 4) * TB:(oc % 4 + 1) * TB]
                                 ).then_inc(stS[oc % 4].h, 16)
            for oc in range(NK):
                stS[oc % 4].n += 16

            ctx.barrier(("tensor", "vector", "sync"))

        if DEBUG:
            @block.gpsimd
            def _(g):
                g.wait_ge(ctx.sem("bar").h, ctx.sem("bar").n)
                g.dma_start(out=dbg_rows[:, :], in_=rows[:]).then_inc(st.h, 16)
                g.dma_start(out=dbg_F[:, :], in_=Fb[:]).then_inc(st.h, 16)
                g.dma_start(out=dbg_xh[:, :], in_=xh[:]).then_inc(st.h, 16)
            st.n += 48

        @block.sync
        def _(sy):
            st.wait(sy)
            for s in stS:
                s.wait(sy)

    return nc



def _shard_tokens_fm(x_flat):
    return [np.ascontiguousarray(x_flat[i * T:(i + 1) * T].T) for i in range(NCORES)]


def _unshard_fm(outs):
    return np.concatenate([o.T for o in outs], axis=0).reshape(2, 8192, D)


def run_l0ffn(shards, inputs):
    nc = build_l0ffn()
    factors = np.ascontiguousarray(inputs["L0_ffn_factors"], dtype=np.float32)
    bias = np.ascontiguousarray(inputs["L0_ffn_bias"], dtype=np.float32).reshape(1, D)
    cst = consts_for_core()
    in_maps = []
    for c in range(NCORES):
        m = {"res": shards[c], "factors": factors, "bias_row": bias}
        m.update(cst)
        in_maps.append(m)
    r = run_bass_kernel_spmd(nc, in_maps, core_ids=list(range(NCORES)))
    return [r.results[c]["res_out"] for c in range(NCORES)]


def kernel(**inputs):
    x = np.asarray(inputs["L0_attn_a_x"], dtype=np.float32).reshape(16384, D)
    shards = _shard_tokens_fm(x)
    shards = phase_l0_attn_a(shards, inputs)
    shards = run_l0ffn(shards, inputs)
    shards = run_l0attnb(shards, inputs)
    shards = run_l0ffn(shards, inputs)
    shards = phase_l1_attn_a(shards, inputs)
    D_bf = run_cast_slab(make_D_all(inputs))
    xhat, lnrows = run_ln_xhat(shards, want_rows=True)
    shards = run_l1ffn(shards, xhat, lnrows, inputs, D_bf)
    xhat = run_ln_xhat(shards)
    shards = run_l1attnb(shards, xhat, inputs)
    xhat, lnrows = run_ln_xhat(shards, want_rows=True)
    shards = run_l1ffn(shards, xhat, lnrows, inputs, D_bf)
    return _unshard_fm(shards)


def phase_l1_attn_a(shards, inputs):
    _, xln = run_ln_xhat(shards, want_f32=True)
    o = run_l1attna_o(xln, inputs)
    wout = np.ascontiguousarray(inputs["L1_attn_a_w_out"], dtype=np.float32)
    return run_addmm(shards, o, wout)



def attn_consts():
    c = {}
    pos = np.arange(8192, dtype=np.float64)[:, None]
    dims = np.arange(64, dtype=np.float64)[None, :]
    ang = np.power(10000.0, -dims / 64.0)
    rad = (pos * ang).astype(np.float32).astype(np.float64)
    c["c_ropecos"] = np.ascontiguousarray(np.cos(rad).T.astype(BF))
    c["c_ropesin"] = np.ascontiguousarray(np.sin(rad).T.astype(BF))
    p = np.arange(128)[:, None]
    cidx = np.arange(512)[None, :]
    amask = np.zeros((128, 4, 512), dtype=np.float32)
    bmask = np.zeros((128, 4, 512), dtype=np.float32)
    for r in range(4):
        amask[:, r, :] = np.where(cidx <= r * 128 + p, 0.0, -1e30)
        bmask[:, r, :] = np.where(cidx >= r * 128 + p, 0.0, -1e30)
    c["c_amask"] = amask.reshape(128, 4 * 512).astype(BF)
    c["c_bmask"] = bmask.reshape(128, 4 * 512).astype(BF)
    c["c_ident_f"] = np.eye(128, dtype=np.float32)
    return c


_ATTN_CONST_SPECS = [
    ("c_ropecos", (64, 8192), BF16),
    ("c_ropesin", (64, 8192), BF16),
    ("c_amask", (128, 4 * 512), BF16),
    ("c_bmask", (128, 4 * 512), BF16),
    ("c_ident_f", (128, 128), F32),
]

LN_MULT = float(-0.25 * np.log(128.0))


def build_attn_a():
    TQ = 8192
    NJ = TQ // 512
    nc = bass.Bass("TRN2", target_bir_lowering=False)
    xhat = nc.dram_tensor("xhat", [D, TQ], BF16, kind="ExternalInput").ap()
    wq = nc.dram_tensor("wq", [D, 512], F32, kind="ExternalInput").ap()
    wk = nc.dram_tensor("wk", [D, 128], F32, kind="ExternalInput").ap()
    wv = nc.dram_tensor("wv", [D, 128], F32, kind="ExternalInput").ap()
    gq = nc.dram_tensor("gq", [128, 4], F32, kind="ExternalInput").ap()
    gk = nc.dram_tensor("gk", [128, 1], F32, kind="ExternalInput").ap()
    o_out = nc.dram_tensor("o_out", [512, TQ], BF16, kind="ExternalOutput").ap()
    cst = declare_consts(nc)
    for name, shape, dt in _ATTN_CONST_SPECS:
        cst[name] = nc.dram_tensor(name, list(shape), dt, kind="ExternalInput").ap()

    with ExitStack() as stack:
        sb = lambda *a, **k: stack.enter_context(nc.sbuf_tensor(*a, **k))
        ps = lambda *a, **k: stack.enter_context(nc.psum_tensor(*a, **k))
        qf = sb("qf", [128, 4 * TQ], BF16)
        kf = sb("kf", [128, TQ], BF16)
        vsb = sb("vsb", [128, TQ], BF16)
        xin = sb("xin", [128, 2 * NK * 512], BF16)
        wq_sb = sb("wq_sb", [128, NK * 512], BF16)
        wk_sb = sb("wk_sb", [128, NK * 128], BF16)
        wv_sb = sb("wv_sb", [128, NK * 128], BF16)
        gq_sb = sb("gq_sb", [128, 4], F32)
        gk_sb = sb("gk_sb", [128, 1], F32)
        cos_sb = sb("cos_sb", [128, 2 * 512], BF16)
        sin_sb = sb("sin_sb", [128, 2 * 512], BF16)
        amask = sb("amask", [128, 4 * 512], BF16)
        bmask = sb("bmask", [128, 4 * 512], BF16)
        identf = sb("identf", [128, 128], F32)
        ones_bf = sb("ones_bf", [1, 128], BF16)
        onescol_bf = sb("onescol_bf", [128, 1], BF16)
        ones_f = sb("ones_f", [128, 1], F32)
        ones_f_row = sb("ones_f_row", [1, 128], F32)
        q2 = sb("q2", [128, 512], BF16)
        t0 = sb("t0", [128, 512], BF16)
        t1 = sb("t1", [128, 512], BF16)
        r0 = sb("r0", [128, 512], BF16)
        r1 = sb("r1", [128, 512], BF16)
        tsw = sb("tsw", [128, 512], BF16)
        rowf = sb("rowf", [1, 512], F32)
        rowb = sb("rowb", [1, 2 * 512], BF16)
        mcols = sb("mcols", [128, 64], F32)
        mT = sb("mT", [64, 128], F32)
        negm_b = sb("negm_b", [1, TQ], BF16)
        tmax = sb("tmax", [128, 2], F32)
        psb = sb("psb", [128, 2 * 512], BF16)
        rl_row = sb("rl_row", [1, 512], F32)
        rlrep = sb("rlrep", [128, 512], F32)
        osb = sb("osb", [128, 2 * 512], BF16)
        cpool = sb("cpool", [128, 4], F32)
        dmy = sb("dmy", [1, 4], F32)
        psP = ps("psP", [128, 1024], F32)
        psV = ps("psV", [128, 1024], F32)
        psS = ps("psS", [128, 1024], F32)
        psRowA = ps("psRowA", [1, 512], F32)
        psRowB = ps("psRowB", [1, 512], F32)
        block = stack.enter_context(nc.Block())

        ctx = Ctx(nc, block, dummy=dmy)
        ld, ldg, mm, ev, ac, st = (ctx.sem(x) for x in ("ld", "ldg", "mm", "ev", "ac", "st"))
        ctx.init_cpool(cpool, [1e-6, LN_MULT])

        @block.sync
        def _(sy):
            for nm, tile in [("c_ones_bf_1x128", ones_bf), ("c_ones_f_128x1", ones_f), ("c_ones_f_1x128", ones_f_row),
                             ("c_amask", amask), ("c_bmask", bmask), ("c_ident_f", identf)]:
                ld.dma(sy.dma_start(out=tile[:], in_=cst[nm][:, :]))

        @block.gpsimd
        def _(g):
            for k in range(NK):
                ldg.dma(g.dma_start(out=wq_sb[:, k * 512:(k + 1) * 512], in_=wq[k * 128:(k + 1) * 128, :]))
                ldg.dma(g.dma_start(out=wk_sb[:, k * 128:(k + 1) * 128], in_=wk[k * 128:(k + 1) * 128, :]))
                ldg.dma(g.dma_start(out=wv_sb[:, k * 128:(k + 1) * 128], in_=wv[k * 128:(k + 1) * 128, :]))
            ldg.dma(g.dma_start(out=gq_sb[:], in_=gq[:, :]))
            ldg.dma(g.dma_start(out=gk_sb[:], in_=gk[:, :]))
            ldg.dma(g.dma_start(out=onescol_bf[:], in_=cst["c_ones_bf_1x128"][:, :].rearrange("a b -> b a")))

        ctx.barrier()

        mm0, ac0, ev0 = mm.n, ac.n, ev.n
        MMJ, ACJ, EVJ = 154, 14, 15

        def mm_proj(q):
            J, hh = divmod(q, 5)
            return mm0 + J * MMJ + hh * 18 + 16

        def mm_ones(q):
            return mm_proj(q) + 1

        def mm_rep(q):
            return mm_proj(q) + 2

        def ac_sq(q):
            J, hh = divmod(q, 5)
            return ac0 + J * ACJ + hh * 2 + 1

        def ac_rstd(q):
            return ac_sq(q) + 1

        def ev_TS(q):
            J, hh = divmod(q, 5)
            return ev0 + J * EVJ + hh * 3 + 1

        def ev_TT(q):
            return ev_TS(q) + 1

        def ev_rope(q):
            return ev_TS(q) + 2

        def mm_v(J, tt):
            return mm0 + J * MMJ + 90 + (tt + 1) * 16

        def ac_vevac(J, tt):
            return ac0 + J * ACJ + 10 + tt + 1

        xsl = ctx.slot_sems("xsl", 2)

        for J in range(NJ):
            xslot = J % 2

            @block.sync
            def _(sy, J=J, xslot=xslot):
                if J >= 2:
                    sy.wait_ge(mm.h, mm0 + (J - 1) * MMJ)
                    sy.wait_ge(ev.h, ev0 + (J - 1) * EVJ)
                for k in range(NK):
                    sy.dma_start(out=xin[:, (xslot * NK + k) * 512:(xslot * NK + k + 1) * 512],
                                 in_=xhat[k * 128:(k + 1) * 128, J * 512:(J + 1) * 512]
                                 ).then_inc(xsl[xslot].h, 16)
                for half in range(2):
                    sy.dma_start(out=cos_sb[64 * half:64 * half + 64, xslot * 512:(xslot + 1) * 512],
                                 in_=cst["c_ropecos"][:, J * 512:(J + 1) * 512]).then_inc(xsl[xslot].h, 16)
                    sy.dma_start(out=sin_sb[64 * half:64 * half + 64, xslot * 512:(xslot + 1) * 512],
                                 in_=cst["c_ropesin"][:, J * 512:(J + 1) * 512]).then_inc(xsl[xslot].h, 16)
                xsl[xslot].n += (NK + 4) * 16

            for hh in range(5):
                q = J * 5 + hh
                pslot = q % 2
                is_k = (hh == 4)
                gcol = gk_sb[:, 0:1] if is_k else gq_sb[:, hh:hh + 1]
                dest = kf if is_k else qf
                dbase = (J * 512) if is_k else (hh * TQ + J * 512)

                @block.tensor
                def _(t, J=J, hh=hh, q=q, pslot=pslot, xslot=xslot, is_k=is_k):
                    t.wait_ge(xsl[xslot].h, 16 * (NK + 4) * (J // 2 + 1))
                    if q >= 2:
                        t.wait_ge(ev.h, ev_TS(q - 2))
                    if q >= 1:
                        t.wait_ge(ac.h, ac_rstd(q - 1))
                    for k in range(NK):
                        wt = wk_sb[:, k * 128:(k + 1) * 128] if is_k else \
                            wq_sb[:, k * 512 + hh * 128: k * 512 + hh * 128 + 128]
                        mm.inc(t.matmul(psP[:, pslot * 512:(pslot + 1) * 512], lhsT=wt,
                                        rhs=xin[:, (xslot * NK + k) * 512:(xslot * NK + k + 1) * 512],
                                        start=(k == 0), stop=(k == NK - 1), skip_group_check=True))

                @block.scalar
                def _(s, q=q, pslot=pslot):
                    s.wait_ge(mm.h, mm_proj(q))
                    if q >= 1:
                        s.wait_ge(mm.h, mm_ones(q - 1))
                    ac.inc(s.activation(q2[:, 0:512],
                                        psP[:, pslot * 512:(pslot + 1) * 512], AF.Square))

                @block.tensor
                def _(t, q=q, pslot=pslot):
                    t.wait_ge(ac.h, ac_sq(q))
                    mm.inc(t.matmul(psRowA[:, :], lhsT=onescol_bf[:],
                                    rhs=q2[:, 0:512],
                                    start=True, stop=True, skip_group_check=True))

                @block.scalar
                def _(s, q=q, pslot=pslot):
                    s.wait_ge(mm.h, mm_ones(q))
                    s.activation(rowf[:, 0:512], psRowA[:, :],
                                 AF.Ln, bias=1e-6, scale=1.0 / 128.0)
                    s.activation(rowf[:, 0:512], rowf[:, 0:512], AF.Exp,
                                 scale=-0.5, bias=LN_MULT)
                    ac.inc(s.copy(rowb[:, pslot * 512:(pslot + 1) * 512], rowf[:, 0:512]))

                @block.tensor
                def _(t, q=q, pslot=pslot):
                    t.wait_ge(ac.h, ac_rstd(q))
                    if q >= 2:
                        t.wait_ge(ev.h, ev_TT(q - 2))
                    mm.inc(t.matmul(psS[:, pslot * 512:(pslot + 1) * 512], lhsT=ones_bf[:],
                                    rhs=rowb[:, pslot * 512:(pslot + 1) * 512],
                                    start=True, stop=True, skip_group_check=True))

                @block.vector
                def _(v, J=J, q=q, pslot=pslot, gcol=gcol, dest=dest, dbase=dbase):
                    v.wait_ge(mm.h, mm_rep(q))
                    sl = slice(0, 512)
                    psl = slice(pslot * 512, (pslot + 1) * 512)
                    ev.inc(v.tensor_scalar(t0[:, sl], psP[:, psl], gcol, None, ALU.mult))
                    ev.inc(v.tensor_tensor(t1[:, sl], t0[:, sl], psS[:, psl], ALU.mult))
                    csl = slice((J % 2) * 512, (J % 2 + 1) * 512)
                    v.tensor_copy(tsw[0:64, sl], t1[64:128, sl])
                    v.tensor_copy(tsw[64:128, sl], t1[0:64, sl])
                    v.tensor_tensor(r0[:, sl], t1[:, sl], cos_sb[:, csl], ALU.mult)
                    v.tensor_tensor(r1[:, sl], tsw[:, sl], sin_sb[:, csl], ALU.mult)
                    v.tensor_tensor(dest[0:64, dbase:dbase + 512], r0[0:64, sl], r1[0:64, sl], ALU.subtract)
                    ev.inc(v.tensor_tensor(dest[64:128, dbase:dbase + 512], r0[64:128, sl],
                                           r1[64:128, sl], ALU.add))

            @block.tensor
            def _(t, J=J, xslot=xslot):
                for tt in range(4):
                    vslot = tt % 2
                    if J >= 1 and tt < 2:
                        t.wait_ge(ac.h, ac_vevac(J - 1, tt + 2))
                    if tt >= 2:
                        t.wait_ge(ac.h, ac_vevac(J, tt - 2))
                    for k in range(NK):
                        mm.inc(t.matmul(psV[:, vslot * 512: vslot * 512 + 128],
                                        lhsT=xin[:, (xslot * NK + k) * 512 + tt * 128:
                                                 (xslot * NK + k) * 512 + tt * 128 + 128],
                                        rhs=wv_sb[:, k * 128:(k + 1) * 128],
                                        start=(k == 0), stop=(k == NK - 1), skip_group_check=True))

            @block.scalar
            def _(s, J=J):
                for tt in range(4):
                    vslot = tt % 2
                    s.wait_ge(mm.h, mm_v(J, tt))
                    ac.inc(s.copy(vsb[:, (J * 4 + tt) * 128:(J * 4 + tt + 1) * 128],
                                  psV[:, vslot * 512: vslot * 512 + 128]))

        ctx.barrier()

        last_ac_rl = [0]
        mrow = ctx.sem("mrow")
        osl = ctx.slot_sems("osl", 2)
        sctr = [0]
        mmS0, evS0 = mm.n, ev.n
        smm = []
        sev = []

        for h in range(4):
            _acn, _evn = ac.n, ev.n

            @block.tensor
            def _(t, _acn=_acn, _evn=_evn):
                t.wait_ge(ac.h, _acn)
                t.wait_ge(ev.h, _evn)

            for jq in range(64):
                ntile = jq // 4 + 1
                r = jq % 4
                for i5 in range(ntile):
                    i = sctr[0]
                    sslot = i % 2
                    partial = (i5 == ntile - 1)

                    @block.tensor
                    def _(t, h=h, jq=jq, i5=i5, i=i, sslot=sslot):
                        if i >= 2:
                            t.wait_ge(ev.h, sev[i - 2])
                        mm.inc(t.matmul(psS[:, sslot * 512:(sslot + 1) * 512],
                                        lhsT=qf[:, h * TQ + jq * 128: h * TQ + jq * 128 + 128],
                                        rhs=kf[:, i5 * 512:(i5 + 1) * 512],
                                        start=True, stop=True, skip_group_check=True))
                    smm.append(mm.n)

                    @block.vector
                    def _(v, jq=jq, i5=i5, i=i, sslot=sslot, partial=partial, r=r):
                        v.wait_ge(mm.h, smm[i])
                        sl = slice(sslot * 512, (sslot + 1) * 512)
                        if partial:
                            v.tensor_tensor(psS[:, sl], psS[:, sl],
                                            amask[:, r * 512:(r + 1) * 512], ALU.add)
                        if i5 == 0:
                            ev.inc(v.reduce_max(mcols[:, jq:jq + 1], psS[:, sl], axis=AX.X))
                        else:
                            v.reduce_max(tmax[:, 0:1], psS[:, sl], axis=AX.X)
                            ev.inc(v.tensor_max(mcols[:, jq:jq + 1], mcols[:, jq:jq + 1],
                                                tmax[:, 0:1]))
                    sev.append(ev.n)
                    sctr[0] += 1

            mmt = mm.n

            @block.tensor
            def _(t, h=h):
                t.wait_ge(ev.h, ev.n)
                mm.inc(t.transpose(psV[0:64, 0:128], mcols[:], identf[:]))

            act = ac.n

            @block.scalar
            def _(s, mmt=mmt):
                s.wait_ge(mm.h, mmt + 1)
                ac.inc(s.activation(mT[:], psV[0:64, 0:128], AF.Copy, scale=-1.0))

            @block.gpsimd
            def _(g, act=act, h=h):
                g.wait_ge(ac.h, act + 1)
                g.dma_start(out=negm_b[0:1, :].rearrange("a (j p) -> a j p", j=64),
                            in_=mT[:]).then_inc(mrow.h, 16)
            mrow.n += 16

            @block.scalar
            def _(s, h=h):
                s.wait_ge(mrow.h, mrow.n)
                ac.inc(s.mul(dmy[:, 3:4], dmy[:, 3:4], 1.0))
            ac_negm = ac.n

            for J in range(NJ):
                vJslot = J % 2
                nI = 4 * J + 4
                pmm = []
                pac = []
                pev_mask = []

                ev_at_blockstart = ev.n
                for I in range(nI):
                    pslot = (I % 2)

                    @block.tensor
                    def _(t, h=h, J=J, I=I, pslot=pslot, ac_negm=ac_negm):
                        t.wait_ge(ac.h, ac_negm)
                        if I >= 2:
                            t.wait_ge(ac.h, pac[I - 2])
                        mm.inc(t.matmul(psP[:, pslot * 512:(pslot + 1) * 512],
                                        lhsT=kf[:, I * 128:(I + 1) * 128],
                                        rhs=qf[:, h * TQ + J * 512: h * TQ + (J + 1) * 512],
                                        start=True, stop=False, skip_group_check=True))
                        mm.inc(t.matmul(psP[:, pslot * 512:(pslot + 1) * 512],
                                        lhsT=ones_bf[:],
                                        rhs=negm_b[:, J * 512:(J + 1) * 512],
                                        start=False, stop=True, skip_group_check=True))
                    pmm.append(mm.n)

                    diag = (I >= 4 * J)
                    if diag:
                        @block.vector
                        def _(v, I=I, J=J, pslot=pslot):
                            v.wait_ge(mm.h, pmm[I])
                            rr = I - 4 * J
                            ev.inc(v.tensor_tensor(psP[:, pslot * 512:(pslot + 1) * 512],
                                                   psP[:, pslot * 512:(pslot + 1) * 512],
                                                   bmask[:, rr * 512:(rr + 1) * 512], ALU.add))
                    pev_mask.append(ev.n)

                    @block.scalar
                    def _(s, I=I, pslot=pslot, diag=diag):
                        if diag:
                            s.wait_ge(ev.h, pev_mask[I])
                        else:
                            s.wait_ge(mm.h, pmm[I])
                        if I >= 2:
                            s.wait_ge(mm.h, pmm[I - 2])
                        ac.inc(s.activation(psb[:, pslot * 512:(pslot + 1) * 512],
                                            psP[:, pslot * 512:(pslot + 1) * 512], AF.Exp))
                    pac.append(ac.n)

                    @block.tensor
                    def _(t, I=I, pslot=pslot, vJslot=vJslot, nI=nI,
                          ev_bs=ev_at_blockstart, ac_rl_prev=last_ac_rl[0]):
                        t.wait_ge(ac.h, pac[I])
                        if I == 0:
                            t.wait_ge(ev.h, ev_bs)
                            t.wait_ge(ac.h, ac_rl_prev)
                        mm.inc(t.matmul(psRowB[:, :], lhsT=onescol_bf[:],
                                        rhs=psb[:, pslot * 512:(pslot + 1) * 512],
                                        start=(I == 0), stop=(I == nI - 1), skip_group_check=True))
                        mm.inc(t.matmul(psV[:, vJslot * 512:(vJslot + 1) * 512],
                                        lhsT=vsb[:, I * 128:(I + 1) * 128],
                                        rhs=psb[:, pslot * 512:(pslot + 1) * 512],
                                        start=(I == 0), stop=(I == nI - 1), skip_group_check=True))
                    pmm[I] = mm.n

                mm_lpv = mm.n

                @block.scalar
                def _(s, mm_lpv=mm_lpv):
                    s.wait_ge(mm.h, mm_lpv)
                    s.activation(rl_row[:], psRowB[:, :], AF.Ln)
                    ac.inc(s.activation(rl_row[:], rl_row[:], AF.Exp, scale=-1.0))
                ac_rl = ac.n
                last_ac_rl[0] = ac_rl

                @block.tensor
                def _(t, ac_rl=ac_rl):
                    t.wait_ge(ac.h, ac_rl)
                    mm.inc(t.matmul(psS[:, 0:512], lhsT=ones_f_row[:], rhs=rl_row[:],
                                    start=True, stop=True, skip_group_check=True))
                mm_rl = mm.n

                @block.scalar
                def _(s, mm_rl=mm_rl):
                    s.wait_ge(mm.h, mm_rl)
                    ac.inc(s.copy(rlrep[:], psS[:, 0:512]))
                ac_rlrep = ac.n

                oslot = (h * NJ + J) % 2

                @block.vector
                def _(v, h=h, J=J, vJslot=vJslot, oslot=oslot, ac_rlrep=ac_rlrep):
                    v.wait_ge(ac.h, ac_rlrep)
                    v.wait_ge(osl[oslot].h, osl[oslot].n)
                    ev.inc(v.tensor_tensor(osb[:, oslot * 512:(oslot + 1) * 512],
                                           psV[:, vJslot * 512:(vJslot + 1) * 512],
                                           rlrep[:], ALU.mult))
                evo = ev.n

                @block.sync
                def _(sy, h=h, J=J, oslot=oslot, evo=evo):
                    sy.wait_ge(ev.h, evo)
                    sy.dma_start(out=o_out[h * 128:(h + 1) * 128, J * 512:(J + 1) * 512],
                                 in_=osb[:, oslot * 512:(oslot + 1) * 512]
                                 ).then_inc(osl[oslot].h, 16)
                osl[oslot].n += 16

        @block.sync
        def _(sy):
            for sem in osl:
                sem.wait(sy)

    return nc


def run_attn_a(xhat_cores, inputs):
    nc = build_attn_a()
    cst = consts_for_core()
    cst.update(attn_consts())
    wq = np.ascontiguousarray(inputs["L0_attn_a_wq"], dtype=np.float32)
    wk = np.ascontiguousarray(inputs["L0_attn_a_wk"], dtype=np.float32)
    wv = np.ascontiguousarray(inputs["L0_attn_a_wv"], dtype=np.float32)
    gq = np.asarray(inputs["L0_attn_a_gq"], dtype=np.float32)
    gk = np.asarray(inputs["L0_attn_a_gk"], dtype=np.float32)
    xb = [np.concatenate([xhat_cores[4 * b + i] for i in range(4)], axis=1) for b in range(2)]
    in_maps = []
    for c in range(NCORES):
        b, g = c // 4, c % 4
        m = {
            "xhat": xb[b],
            "wq": np.ascontiguousarray(wq[:, g * 512:(g + 1) * 512]),
            "wk": np.ascontiguousarray(wk[:, g * 128:(g + 1) * 128]),
            "wv": np.ascontiguousarray(wv[:, g * 128:(g + 1) * 128]),
            "gq": np.ascontiguousarray(gq[g].T),
            "gk": np.ascontiguousarray(gk[g].reshape(128, 1)),
        }
        m.update(cst)
        in_maps.append(m)
    r = run_bass_kernel_spmd(nc, in_maps, core_ids=list(range(NCORES)))
    return [r.results[c]["o_out"] for c in range(NCORES)]



def build_ln_xhat(f32=False):
    nc = bass.Bass("TRN2", target_bir_lowering=False)
    res = nc.dram_tensor("res", [D, T], F32, kind="ExternalInput").ap()
    xhat_out = nc.dram_tensor("xhat_out", [D, T], BF16, kind="ExternalOutput").ap()
    xf_out = (nc.dram_tensor("xf_out", [D, T], F32, kind="ExternalOutput").ap()
              if f32 else None)
    lnrows_out = nc.dram_tensor("lnrows_out", [1, 2 * T], F32, kind="ExternalOutput").ap()
    cst = declare_consts(nc)

    with ExitStack() as stack:
        sb = lambda *a, **k: stack.enter_context(nc.sbuf_tensor(*a, **k))
        ps = lambda *a, **k: stack.enter_context(nc.psum_tensor(*a, **k))
        xt = sb("xt", [128, NK * TB], F32)
        sq = sb("sq", [128, 2 * TB], F32)
        xh = sb("xh", [128, 4 * TB], BF16)
        xf = sb("xf", [128, 4 * TB], F32) if f32 else None
        rows = sb("rows", [1, 6 * D], F32)
        rows_b = sb("rows_b", [1, 2 * D], BF16)
        ones_bf = sb("ones_bf", [1, 128], BF16)
        ones_f = sb("ones_f", [128, 1], F32)
        cpool = sb("cpool", [128, 4], F32)
        dmy = sb("dmy", [1, 4], F32)
        psA = ps("psA", [1, 512], F32)
        psB = ps("psB", [1, 512], F32)
        psRep = ps("psRep", [128, 2 * 512], F32)
        block = stack.enter_context(nc.Block())

        ctx = Ctx(nc, block, dummy=dmy)
        ld, mm, ev, ac, st = (ctx.sem(x) for x in ("ld", "mm", "ev", "ac", "st"))
        ctx.init_cpool(cpool, [1e-5])
        ctx.barrier(("vector", "scalar", "tensor"))

        @block.sync
        def _(sy):
            ld.dma(sy.dma_start(out=ones_bf[:], in_=cst["c_ones_bf_1x128"][:, :]))
            ld.dma(sy.dma_start(out=ones_f[:], in_=cst["c_ones_f_128x1"][:, :]))
        ctx.barrier(("sync", "tensor", "scalar", "vector"))

        emit_ln_rows(ctx, res, xt, sq, rows, rows_b, ones_f, psA, psB)

        @block.sync
        def _(sy):
            st.dma(sy.dma_start(out=lnrows_out[:, 0:T], in_=rows[:, 2 * D:3 * D]))
            st.dma(sy.dma_start(out=lnrows_out[:, T:2 * T], in_=rows[:, 4 * D:5 * D]))

        xsl = ctx.slot_sems("xhst", 4)
        for tb in range(NTB):
            @block.sync
            def _(sy, tb=tb):
                for k in range(NK):
                    sy.dma_start(out=xt[:, k * TB:(k + 1) * TB],
                                 in_=res[k * 128:(k + 1) * 128, tb * TB:(tb + 1) * TB]).then_inc(ld.h, 16)
            ld.n += NK * 16

            mm1 = mm.n

            @block.tensor
            def _(t, tb=tb):
                if tb > 0:
                    t.wait_ge(ev.h, ev.n)
                mm.inc(t.matmul(psRep[:, 0:512], lhsT=ones_bf[:],
                                rhs=rows_b[:, 0 * D + tb * TB: 0 * D + (tb + 1) * TB],
                                start=True, stop=True))
                mm.inc(t.matmul(psRep[:, 512:1024], lhsT=ones_bf[:],
                                rhs=rows_b[:, 1 * D + tb * TB: 1 * D + (tb + 1) * TB],
                                start=True, stop=True))

            ev1 = ev.n

            @block.vector
            def _(v, tb=tb, mm1=mm1):
                v.wait_ge(mm.h, mm1 + 2)
                ld.wait(v)
                for k in range(NK):
                    sl = (k % 2) * TB
                    ks = k % 4
                    v.wait_ge(xsl[ks].h, xsl[ks].n)
                    v.tensor_tensor(sq[:, sl:sl + TB], xt[:, k * TB:(k + 1) * TB],
                                    psRep[:, 0:512], ALU.mult)
                    if f32:
                        v.tensor_tensor(xf[:, ks * TB:(ks + 1) * TB], sq[:, sl:sl + TB],
                                        psRep[:, 512:1024], ALU.subtract)
                    ev.inc(v.tensor_tensor(xh[:, ks * TB:(ks + 1) * TB], sq[:, sl:sl + TB],
                                           psRep[:, 512:1024], ALU.subtract))

            @block.sync
            def _(sy, tb=tb, ev1=ev1):
                for k in range(NK):
                    ks = k % 4
                    sy.wait_ge(ev.h, ev1 + k + 1)
                    sy.dma_start(out=xhat_out[k * 128:(k + 1) * 128, tb * TB:(tb + 1) * TB],
                                 in_=xh[:, ks * TB:(ks + 1) * TB]).then_inc(xsl[ks].h, 16)
                    if f32:
                        sy.dma_start(out=xf_out[k * 128:(k + 1) * 128, tb * TB:(tb + 1) * TB],
                                     in_=xf[:, ks * TB:(ks + 1) * TB]).then_inc(xsl[ks].h, 16)
            for k in range(NK):
                xsl[k % 4].n += 32 if f32 else 16

        @block.sync
        def _(sy):
            st.wait(sy)
            for sem in xsl:
                sem.wait(sy)
    return nc


def build_addmm():
    nc = bass.Bass("TRN2", target_bir_lowering=False)
    res = nc.dram_tensor("res", [D, T], F32, kind="ExternalInput").ap()
    src = nc.dram_tensor("src", [D, T], BF16, kind="ExternalInput").ap()
    W = nc.dram_tensor("W", [D, D], F32, kind="ExternalInput").ap()
    res_out = nc.dram_tensor("res_out", [D, T], F32, kind="ExternalOutput").ap()
    cst = declare_consts(nc)

    with ExitStack() as stack:
        sb = lambda *a, **k: stack.enter_context(nc.sbuf_tensor(*a, **k))
        ps = lambda *a, **k: stack.enter_context(nc.psum_tensor(*a, **k))
        Wb = sb("Wb", [128, NK * D], BF16)
        xt = sb("xt", [128, NK * TB], F32)
        sh = sb("sh", [128, NK * TB], BF16)
        outt = sb("outt", [128, 4 * TB], F32)
        dmy = sb("dmy", [1, 4], F32)
        psO = ps("psO", [128, 2 * 512], F32)
        block = stack.enter_context(nc.Block())

        ctx = Ctx(nc, block, dummy=dmy)
        ld, ldg, mm, ev = (ctx.sem(x) for x in ("ld", "ldg", "mm", "ev"))
        stS = ctx.slot_sems("stS", 4)

        @block.gpsimd
        def _(g):
            for k in range(NK):
                g.dma_start(out=Wb[:, k * D:(k + 1) * D],
                            in_=W[k * 128:(k + 1) * 128, :]).then_inc(ldg.h, 16)
        ldg.n += NK * 16

        for tb in range(NTB):
            @block.sync
            def _(sy, tb=tb):
                if tb >= 1:
                    sy.wait_ge(mm.h, mm.n)
                for k in range(NK):
                    sy.dma_start(out=xt[:, k * TB:(k + 1) * TB],
                                 in_=res[k * 128:(k + 1) * 128, tb * TB:(tb + 1) * TB]).then_inc(ld.h, 16)
                    sy.dma_start(out=sh[:, k * TB:(k + 1) * TB],
                                 in_=src[k * 128:(k + 1) * 128, tb * TB:(tb + 1) * TB]).then_inc(ld.h, 16)
            ld.n += 2 * NK * 16

            mm2 = mm.n
            evo = ev.n

            @block.tensor
            def _(t, tb=tb, evo=evo):
                ld.wait(t)
                ldg.wait(t)
                for oc in range(NK):
                    pb = oc % 2
                    if oc >= 2:
                        t.wait_ge(ev.h, evo + (oc - 2) + 1)
                    for k in range(NK):
                        mm.inc(t.matmul(psO[:, pb * 512:(pb + 1) * 512],
                                        lhsT=Wb[:, k * D + oc * 128: k * D + oc * 128 + 128],
                                        rhs=sh[:, k * TB:(k + 1) * TB],
                                        start=(k == 0), stop=(k == NK - 1), skip_group_check=True))

            stS0 = [x.n for x in stS]

            @block.vector
            def _(v, tb=tb, mm2=mm2, stS0=stS0):
                for oc in range(NK):
                    pb = oc % 2
                    v.wait_ge(mm.h, mm2 + (oc + 1) * NK)
                    if oc >= 4:
                        v.wait_ge(stS[oc % 4].h, stS0[oc % 4] + 16 * (oc // 4))
                    ev.inc(v.tensor_tensor(outt[:, (oc % 4) * TB:(oc % 4 + 1) * TB],
                                           psO[:, pb * 512:(pb + 1) * 512],
                                           xt[:, oc * TB:(oc + 1) * TB], ALU.add))

            @block.sync
            def _(sy, tb=tb, evo=evo):
                for oc in range(NK):
                    sy.wait_ge(ev.h, evo + oc + 1)
                    sy.dma_start(out=res_out[oc * 128:(oc + 1) * 128, tb * TB:(tb + 1) * TB],
                                 in_=outt[:, (oc % 4) * TB:(oc % 4 + 1) * TB]
                                 ).then_inc(stS[oc % 4].h, 16)
            for oc in range(NK):
                stS[oc % 4].n += 16

        @block.sync
        def _(sy):
            for sem in stS:
                sem.wait(sy)
    return nc


def run_ln_xhat(shards, want_rows=False, want_f32=False):
    nc = build_ln_xhat(f32=want_f32)
    cst = consts_for_core()
    in_maps = []
    for c in range(NCORES):
        m = {"res": shards[c]}
        m.update(cst)
        in_maps.append(m)
    r = run_bass_kernel_spmd(nc, in_maps, core_ids=list(range(NCORES)))
    xh = [r.results[c]["xhat_out"] for c in range(NCORES)]
    if want_f32:
        return xh, [r.results[c]["xf_out"] for c in range(NCORES)]
    if want_rows:
        return xh, [r.results[c]["lnrows_out"] for c in range(NCORES)]
    return xh


def run_addmm(shards, srcs, W):
    nc = build_addmm()
    cst = consts_for_core()
    in_maps = []
    for c in range(NCORES):
        m = {"res": shards[c], "src": srcs[c], "W": W}
        m.update(cst)
        in_maps.append(m)
    r = run_bass_kernel_spmd(nc, in_maps, core_ids=list(range(NCORES)))
    return [r.results[c]["res_out"] for c in range(NCORES)]


def phase_l0_attn_a(shards, inputs):
    xhat = run_ln_xhat(shards)
    o_cores = run_attn_a(xhat, inputs)
    wo = np.ascontiguousarray(inputs["L0_attn_a_wo"], dtype=np.float32)
    srcs = []
    for c in range(NCORES):
        b, i = c // 4, c % 4
        srcs.append(np.ascontiguousarray(
            np.concatenate([o_cores[4 * b + g][:, i * 2048:(i + 1) * 2048] for g in range(4)], axis=0)))
    return run_addmm(shards, srcs, wo)



FS_ALL = 8 * 1408 + 5632
NKH = FS_ALL // 128

def build_cast_slab(rows):
    nc = bass.Bass("TRN2", target_bir_lowering=False)
    src = nc.dram_tensor("src", [rows, D], F32, kind="ExternalInput").ap()
    dst = nc.dram_tensor("dst", [rows, D], BF16, kind="ExternalOutput").ap()
    NRT_ = rows // 128
    with ExitStack() as stack:
        sb = lambda *a, **k: stack.enter_context(nc.sbuf_tensor(*a, **k))
        buf = sb("buf", [128, 4 * D], BF16)
        block = stack.enter_context(nc.Block())
        ctx = Ctx(nc, block)
        lsl = ctx.slot_sems("clsl", 4)
        ssl = ctx.slot_sems("cssl", 4)

        @block.gpsimd
        def _(g):
            for i in range(NRT_):
                sl = i % 4
                if i >= 4:
                    g.wait_ge(ssl[sl].h, 16 * (i // 4))
                g.dma_start(out=buf[:, sl * D:(sl + 1) * D],
                            in_=src[i * 128:(i + 1) * 128, :]).then_inc(lsl[sl].h, 16)
                lsl[sl].n += 16

        @block.sync
        def _(sy):
            for i in range(NRT_):
                sl = i % 4
                sy.wait_ge(lsl[sl].h, 16 * (i // 4 + 1))
                sy.dma_start(out=dst[i * 128:(i + 1) * 128, :],
                             in_=buf[:, sl * D:(sl + 1) * D]).then_inc(ssl[sl].h, 16)
                ssl[sl].n += 16
            for sem in ssl:
                sem.wait(sy)
    return nc


def run_cast_slab(arr):
    rows = arr.shape[0]
    assert rows % 128 == 0
    nc = build_cast_slab(rows)
    arr = np.ascontiguousarray(arr)
    in_maps = [{"src": arr} for _ in range(NCORES)]
    r = run_bass_kernel_spmd(nc, in_maps, core_ids=list(range(NCORES)))
    return r.results[0]["dst"]


def build_l1ffn():
    nc = bass.Bass("TRN2", target_bir_lowering=False)
    res = nc.dram_tensor("res", [D, T], F32, kind="ExternalInput").ap()
    xhat = nc.dram_tensor("xhat", [D, T], BF16, kind="ExternalInput").ap()
    lnrows = nc.dram_tensor("lnrows", [1, 2 * T], F32, kind="ExternalInput").ap()
    wr9 = nc.dram_tensor("wr9", [D, 9], F32, kind="ExternalInput").ap()
    G_t = nc.dram_tensor("G_t", [NKH, 128, D], F32, kind="ExternalInput").ap()
    U_t = nc.dram_tensor("U_t", [NKH, 128, D], F32, kind="ExternalInput").ap()
    D_bf = nc.dram_tensor("D_bf", [FS_ALL, D], BF16, kind="ExternalInput").ap()
    res_out = nc.dram_tensor("res_out", [D, T], F32, kind="ExternalOutput").ap()
    Hbuf = nc.dram_tensor("Hbuf", [FS_ALL, T], BF16,
                          kind="ExternalOutput" if DEBUG else "Internal").ap()
    CrowsD = nc.dram_tensor("CrowsD", [9, T], BF16, kind="Internal").ap()
    if DEBUG:
        dbg_C = nc.dram_tensor("dbg_C", [9, T], F32, kind="ExternalOutput").ap()
        dbg_crep = nc.dram_tensor("dbg_crep", [128, 9 * T], F32, kind="ExternalOutput").ap()
    cst = declare_consts(nc)
    cst["c_ident_f"] = nc.dram_tensor("c_ident_f", [128, 128], F32, kind="ExternalInput").ap()

    with ExitStack() as stack:
        sb = lambda *a, **k: stack.enter_context(nc.sbuf_tensor(*a, **k))
        ps = lambda *a, **k: stack.enter_context(nc.psum_tensor(*a, **k))
        xh = sb("xh", [128, NK * T], BF16)
        crow = sb("crow", [1, 2 * T], BF16)
        mrow = sb("mrow", [1, T], F32)
        negm = sb("negm", [1, T], F32)
        rstdc = sb("rstdc", [128, 16], F32)
        csrow = sb("csrow", [1, 9], F32)
        resin = sb("resin", [128, 2 * D], F32)
        crepall = sb("crepall", [128, 9 * T], BF16)
        gstrip = sb("gstrip", [128, 2 * D], BF16)
        ustrip = sb("ustrip", [128, 2 * D], BF16)
        sgl = sb("sgl", [128, 2 * 512], F32)
        hb = sb("hb", [128, 2 * 512], F32)
        hstag = sb("hstag", [128, 2 * 512], BF16)
        htile = sb("htile", [128, 2 * 512], BF16)
        dtile = sb("dtile", [128, 2 * 1024], BF16)
        xt = sb("xt", [128, 8 * 512], F32)
        outt = sb("outt", [128, 4 * 512], F32)
        wr_sb = sb("wr_sb", [128, NK * 9], F32)
        Call = sb("Call", [128, 128], F32)
        sgall = sb("sgall", [128, 16], F32)
        CTs = sb("CTs", [128, 128], BF16)
        sgT = sb("sgT", [16, 128], BF16)
        ones_bf = sb("ones_bf", [1, 128], BF16)
        ones_f = sb("ones_f", [128, 1], F32)
        identf = sb("identf", [128, 128], F32)
        Lall = sb("Lall", [128, 16 * 9], F32)
        rw0 = sb("rw0", [128, 128], F32)
        rw1 = sb("rw1", [128, 128], F32)
        rw2 = sb("rw2", [128, 128], F32)
        rw3 = sb("rw3", [128, 128], F32)
        rw4 = sb("rw4", [128, 128], F32)
        rcols = sb("rcols", [128, 64], F32)
        dmy = sb("dmy", [1, 4], F32)
        ps1 = ps("ps1", [128, 2048], F32)
        ps2 = ps("ps2", [128, 2048], F32)
        block = stack.enter_context(nc.Block())

        ctx = Ctx(nc, block, dummy=dmy)
        ld, ldg, mm, ev, ac, st = (ctx.sem(x) for x in ("ld", "ldg", "mm", "ev", "ac", "st"))

        @block.sync
        def _(sy):
            ld.dma(sy.dma_start(out=ones_bf[:], in_=cst["c_ones_bf_1x128"][:, :]))
            ld.dma(sy.dma_start(out=ones_f[:], in_=cst["c_ones_f_128x1"][:, :]))
            ld.dma(sy.dma_start(out=identf[:], in_=cst["c_ident_f"][:, :]))
            for k in range(NK):
                for tb4 in range(4):
                    ld.dma(sy.dma_start(out=xh[:, k * T + tb4 * 512: k * T + (tb4 + 1) * 512],
                                        in_=xhat[k * 128:(k + 1) * 128, tb4 * 512:(tb4 + 1) * 512]))

        @block.sync
        def _(sy):
            for k in range(NK):
                ld.dma(sy.dma_start(out=wr_sb[:, k * 9:(k + 1) * 9], in_=wr9[k * 128:(k + 1) * 128, :]))
            ld.dma(sy.dma_start(out=mrow[:, :], in_=lnrows[:, 0:T]))
            for tt in range(16):
                ld.dma(sy.dma_start(out=rstdc[:, tt:tt + 1],
                                    in_=lnrows[0:1, T + tt * 128: T + (tt + 1) * 128]
                                    .rearrange("a p -> p a")))

        ctx.barrier()

        @block.tensor
        def _(t):
            for k in range(NK):
                mm.inc(t.matmul(ps2[0:1, 0:9], lhsT=ones_f[:], rhs=wr_sb[:, k * 9:(k + 1) * 9],
                                start=(k == 0), stop=(k == NK - 1), skip_group_check=True))
        mm_cs = mm.n

        @block.vector
        def _(v):
            v.wait_ge(mm.h, mm_cs)
            v.tensor_copy(csrow[:, :], ps2[0:1, 0:9])
            ev.inc(v.tensor_scalar(negm[:, :], mrow[:, :], -1.0, None, ALU.mult))

        ctx.barrier()

        rsl = ctx.slot_sems("rsl", 2)
        for tt in range(16):
            slot = tt % 2

            @block.sync
            def _(sy, tt=tt, slot=slot):
                if tt >= 2:
                    sy.wait_ge(mm.h, mm.n)
                for k in range(NK):
                    sy.dma_start(out=resin[:, slot * D + k * 128: slot * D + (k + 1) * 128],
                                 in_=res[k * 128:(k + 1) * 128, tt * 128:(tt + 1) * 128]
                                 ).then_inc(rsl[slot].h, 16)
                rsl[slot].n += NK * 16

            rt0 = mm.n

            @block.tensor
            def _(t, tt=tt, slot=slot):
                t.wait_ge(rsl[slot].h, rsl[slot].n)
                if tt > 0:
                    t.wait_ge(ev.h, ev.n)
                for k in range(NK):
                    mm.inc(t.matmul(ps1[:, 0:9],
                                    lhsT=resin[:, slot * D + k * 128: slot * D + (k + 1) * 128],
                                    rhs=wr_sb[:, k * 9:(k + 1) * 9],
                                    start=(k == 0), stop=False, skip_group_check=True))
                mm.inc(t.matmul(ps1[:, 0:9],
                                lhsT=negm[0:1, tt * 128:(tt + 1) * 128],
                                rhs=csrow[:, :], start=False, stop=True, skip_group_check=True))

            @block.vector
            def _(v, tt=tt, rt0=rt0):
                v.wait_ge(mm.h, rt0 + NK + 1)
                ev.inc(v.tensor_scalar(Lall[:, tt * 9:(tt + 1) * 9], ps1[:, 0:9],
                                       rstdc[:, tt:tt + 1], None, ALU.mult))

        ctx.barrier()

        l8 = Lall[:].rearrange("p (t n) -> p t n", n=9)[:, :, 0:8]
        sgin = Lall[:].rearrange("p (t n) -> p t n", n=9)[:, :, 8:9]
        v3 = lambda tile: tile[:, 0:128].rearrange("p (t n) -> p t n", n=8)
        bc = lambda col: col[:, 0:16].rearrange("p (t o) -> p t o", o=1).broadcast_to((128, 16, 8))

        @block.vector
        def _(v):
            v.reduce_max(rcols[:, 0:16], l8, axis=AX.X)
            ev.inc(v.tensor_tensor(v3(rw3), l8, bc(rcols[:, 0:16]), ALU.subtract))
        e_d1 = ev.n

        @block.scalar
        def _(s):
            s.wait_ge(ev.h, e_d1)
            s.activation(rw4[:, 0:128], rw3[:, 0:128], AF.Exp)
            ac.inc(s.activation(sgall[:, 0:16], sgin, AF.Sigmoid))
        a_ex = ac.n

        @block.vector
        def _(v):
            v.wait_ge(ac.h, a_ex)
            v.tensor_scalar(rw0[:, 0:128], rw3[:, 0:128], -1.0, None, ALU.mult)
            v.tensor_max(rw1[:, 0:128], rw3[:, 0:128], rw0[:, 0:128])
            v.tensor_scalar(rw0[:, 0:128], rw1[:, 0:128], 1e6, None, ALU.mult)
            v.tensor_scalar_min(rw2[:, 0:128], rw0[:, 0:128], 1.0)
            v.tensor_scalar(rw0[:, 0:128], rw2[:, 0:128], -1.0, 1.0, ALU.mult, ALU.add)
            v.scalar_tensor_tensor(rw1[:, 0:128], rw0[:, 0:128], -5e3, l8, ALU.mult, ALU.add)
            v.reduce_max(rcols[:, 16:32], v3(rw1), axis=AX.X)
            v.tensor_tensor(v3(rw0), l8, bc(rcols[:, 16:32]), ALU.subtract)
            v.tensor_scalar(rw1[:, 0:128], rw0[:, 0:128], -1e6, None, ALU.mult)
            v.tensor_scalar_max(rw2[:, 0:128], rw1[:, 0:128], 0.0)
            v.tensor_scalar_min(rw1[:, 0:128], rw2[:, 0:128], 1.0)
            v.tensor_scalar(rw2[:, 0:128], rw1[:, 0:128], -1.0, 1.0, ALU.mult, ALU.add)
            ev.inc(v.reduce_sum(rcols[:, 32:48], v3(rw4), axis=AX.X))
        e_den = ev.n

        @block.scalar
        def _(s):
            s.wait_ge(ev.h, e_den)
            s.activation(rcols[:, 48:64], rcols[:, 32:48], AF.Ln)
            ac.inc(s.activation(rcols[:, 48:64], rcols[:, 48:64], AF.Exp, scale=-1.0))
        a_rd = ac.n

        @block.vector
        def _(v):
            v.wait_ge(ac.h, a_rd)
            v.tensor_tensor(v3(rw0), v3(rw4), bc(rcols[:, 48:64]), ALU.mult)
            ev.inc(v.tensor_tensor(Call[:, 0:128], rw0[:, 0:128], rw2[:, 0:128], ALU.mult))

        ctx.barrier()

        @block.tensor
        def _(t):
            mm.inc(t.transpose(ps1[0:128, 0:128], Call[:], identf[:]))
            mm.inc(t.transpose(ps2[0:16, 0:128], sgall[:], identf[:]))
        mmT = mm.n

        @block.scalar
        def _(s, mmT=mmT):
            s.wait_ge(mm.h, mmT)
            ac.inc(s.copy(CTs[:], ps1[0:128, 0:128]))
            ac.inc(s.copy(sgT[:], ps2[0:16, 0:128]))
        acT = ac.n

        flat = ctx.sem("flat")

        @block.gpsimd
        def _(g, acT=acT):
            g.wait_ge(ac.h, acT)
            for e in range(8):
                g.dma_start(out=CrowsD[e:e + 1, :].rearrange("a (tile t) -> a tile t", tile=16),
                            in_=CTs[e:128:8, :]).then_inc(flat.h, 16)
            g.dma_start(out=CrowsD[8:9, :].rearrange("a (tile t) -> a tile t", tile=16),
                        in_=sgT[:]).then_inc(flat.h, 16)
        flat.n += 9 * 16

        ctx.barrier()

        crsl = ctx.slot_sems("crsl", 2)
        for e in range(9):
            cslot = e % 2

            @block.gpsimd
            def _(g, e=e, cslot=cslot):
                if e >= 2:
                    g.wait_ge(mm.h, mm.n)
                g.wait_ge(flat.h, flat.n)
                g.dma_start(out=crow[0:1, cslot * T:(cslot + 1) * T],
                            in_=CrowsD[e:e + 1, :]).then_inc(crsl[cslot].h, 16)
                crsl[cslot].n += 16

            for q4 in range(4):
                mmc = mm.n

                @block.tensor
                def _(t, e=e, q4=q4, cslot=cslot):
                    t.wait_ge(crsl[cslot].h, crsl[cslot].n)
                    t.wait_ge(ac.h, ac.n)
                    mm.inc(t.matmul(ps1[:, 0:512], lhsT=ones_bf[:],
                                    rhs=crow[0:1, cslot * T + q4 * 512: cslot * T + (q4 + 1) * 512],
                                    start=True, stop=True, skip_group_check=True))

                @block.scalar
                def _(s, e=e, q4=q4, mmc=mmc):
                    s.wait_ge(mm.h, mmc + 1)
                    ac.inc(s.copy(crepall[:, e * T + q4 * 512: e * T + (q4 + 1) * 512],
                                  ps1[:, 0:512]))

        ctx.barrier()

        echunks = [11] * 8 + [44]
        ebase = [e * 1408 for e in range(8)] + [8 * 1408]
        gsl = ctx.slot_sems("gsl", 2)
        hst = ctx.slot_sems("hst", 2)
        mmS2, acS2, evS2 = mm.n, ac.n, ev.n

        def mm_gu(i, part):
            return mmS2 + i * 32 + part

        def ev_tt(i, j):
            return evS2 + 2 * i + j

        def ac_silu(i):
            return acS2 + i + 1

        eit = 0
        fseq = []
        for e in range(9):
            for fch in range(echunks[e]):
                fseq.append((e, fch))

        for fi, (e, fch) in enumerate(fseq):
            gslot = fi % 2

            @block.gpsimd
            def _(g, fi=fi, e=e, fch=fch, gslot=gslot):
                if fi >= 2:
                    g.wait_ge(mm.h, mm_gu(4 * (fi - 2) + 3, 32))
                kh = ebase[e] // 128 + fch
                g.dma_start(out=gstrip[:, gslot * D:(gslot + 1) * D],
                            in_=G_t[kh, :, :]).then_inc(gsl[gslot].h, 16)
                g.dma_start(out=ustrip[:, gslot * D:(gslot + 1) * D],
                            in_=U_t[kh, :, :]).then_inc(gsl[gslot].h, 16)
                gsl[gslot].n += 32

            for tq in range(4):
                i = eit
                pslot = i % 2

                @block.tensor
                def _(t, fi=fi, i=i, tq=tq, gslot=gslot, pslot=pslot):
                    t.wait_ge(gsl[gslot].h, 32 * (fi // 2 + 1))
                    if i >= 2:
                        t.wait_ge(ev.h, ev_tt(i - 2, 2))
                    for k in range(NK):
                        mm.inc(t.matmul(ps1[:, pslot * 512:(pslot + 1) * 512],
                                        lhsT=gstrip[:, gslot * D + k * 128: gslot * D + (k + 1) * 128],
                                        rhs=xh[:, k * T + tq * 512: k * T + (tq + 1) * 512],
                                        start=(k == 0), stop=(k == NK - 1), skip_group_check=True))
                    for k in range(NK):
                        mm.inc(t.matmul(ps2[:, pslot * 512:(pslot + 1) * 512],
                                        lhsT=ustrip[:, gslot * D + k * 128: gslot * D + (k + 1) * 128],
                                        rhs=xh[:, k * T + tq * 512: k * T + (tq + 1) * 512],
                                        start=(k == 0), stop=(k == NK - 1), skip_group_check=True))

                @block.scalar
                def _(s, i=i, pslot=pslot):
                    s.wait_ge(mm.h, mm_gu(i, 16))
                    if i >= 2:
                        s.wait_ge(ev.h, ev_tt(i - 2, 1))
                    ac.inc(s.activation(sgl[:, pslot * 512:(pslot + 1) * 512],
                                        ps1[:, pslot * 512:(pslot + 1) * 512], AF.Silu))

                @block.vector
                def _(v, i=i, e=e, tq=tq, pslot=pslot):
                    v.wait_ge(ac.h, ac_silu(i))
                    v.wait_ge(mm.h, mm_gu(i, 32))
                    if i >= 2:
                        v.wait_ge(hst[pslot].h, 16 * (i // 2))
                    ev.inc(v.tensor_tensor(hb[:, pslot * 512:(pslot + 1) * 512],
                                           sgl[:, pslot * 512:(pslot + 1) * 512],
                                           ps2[:, pslot * 512:(pslot + 1) * 512], ALU.mult))
                    ev.inc(v.tensor_tensor(hstag[:, pslot * 512:(pslot + 1) * 512],
                                           hb[:, pslot * 512:(pslot + 1) * 512],
                                           crepall[:, e * T + tq * 512: e * T + (tq + 1) * 512],
                                           ALU.mult))

                @block.sync
                def _(sy, i=i, e=e, fch=fch, tq=tq, pslot=pslot):
                    sy.wait_ge(ev.h, ev_tt(i, 2))
                    sy.dma_start(out=Hbuf[ebase[e] + fch * 128: ebase[e] + (fch + 1) * 128,
                                          tq * 512:(tq + 1) * 512],
                                 in_=hstag[:, pslot * 512:(pslot + 1) * 512]
                                 ).then_inc(hst[pslot].h, 16)
                hst[pslot].n += 16
                eit += 1

        ctx.barrier()

        hsl = ctx.slot_sems("hsl", 2)
        dsl = ctx.slot_sems("dsl", 2)
        xsl3 = ctx.slot_sems("xsl3", 8)
        sst = ctx.slot_sems("sst", 4)
        mmS3 = mm.n
        evS3 = ev.n

        for it3 in range(8):
            tb, ohalf = divmod(it3, 2)

            @block.sync
            def _(sy, it3=it3, tb=tb, ohalf=ohalf):
                for oc in range(8):
                    if it3 >= 1:
                        sy.wait_ge(ev.h, evS3 + (it3 - 1) * 8 + oc + 1)
                    goc = ohalf * 8 + oc
                    sy.dma_start(out=xt[:, oc * 512:(oc + 1) * 512],
                                 in_=res[goc * 128:(goc + 1) * 128, tb * 512:(tb + 1) * 512]
                                 ).then_inc(xsl3[oc].h, 16)
                    xsl3[oc].n += 16

            @block.gpsimd
            def _(g, it3=it3, tb=tb, ohalf=ohalf):
                for k in range(NKH):
                    sl = k % 2
                    if k >= 2:
                        g.wait_ge(mm.h, mmS3 + it3 * NKH * 8 + 8 * (k - 1))
                    if it3 >= 1 and k < 2:
                        g.wait_ge(mm.h, mmS3 + it3 * NKH * 8 - 8)
                    g.dma_start(out=htile[:, sl * 512:(sl + 1) * 512],
                                in_=Hbuf[k * 128:(k + 1) * 128, tb * 512:(tb + 1) * 512]
                                ).then_inc(hsl[sl].h, 16)
                    hsl[sl].n += 16
                    g.dma_start(out=dtile[:, sl * 1024:(sl + 1) * 1024],
                                in_=D_bf[k * 128:(k + 1) * 128, ohalf * 1024:(ohalf + 1) * 1024]
                                ).then_inc(dsl[sl].h, 16)
                    dsl[sl].n += 16

            @block.tensor
            def _(t, it3=it3):
                if it3 >= 1:
                    t.wait_ge(ev.h, evS3 + it3 * 8)
                for k in range(NKH):
                    sl = k % 2
                    t.wait_ge(hsl[sl].h, 16 * (it3 * (NKH // 2) + k // 2 + 1))
                    t.wait_ge(dsl[sl].h, 16 * (it3 * (NKH // 2) + k // 2 + 1))
                    for oc in range(8):
                        bank = ps1 if oc < 4 else ps2
                        boff = (oc % 4) * 512
                        mm.inc(t.matmul(bank[:, boff:boff + 512],
                                        lhsT=dtile[:, sl * 1024 + oc * 128: sl * 1024 + (oc + 1) * 128],
                                        rhs=htile[:, sl * 512:(sl + 1) * 512],
                                        start=(k == 0), stop=(k == NKH - 1), skip_group_check=True))

            mm_it3 = mm.n

            sst0 = [x.n for x in sst]

            @block.vector
            def _(v, it3=it3, mm_it3=mm_it3, sst0=sst0):
                v.wait_ge(mm.h, mm_it3)
                for oc in range(8):
                    bank = ps1 if oc < 4 else ps2
                    boff = (oc % 4) * 512
                    v.wait_ge(xsl3[oc].h, xsl3[oc].n)
                    v.wait_ge(sst[oc % 4].h, sst0[oc % 4] + (16 if oc >= 4 else 0))
                    ev.inc(v.tensor_tensor(outt[:, (oc % 4) * 512:(oc % 4 + 1) * 512],
                                           bank[:, boff:boff + 512],
                                           xt[:, oc * 512:(oc + 1) * 512], ALU.add))

            ev_it3 = ev.n - 8

            @block.sync
            def _(sy, it3=it3, tb=tb, ohalf=ohalf, ev_it3=ev_it3):
                for oc in range(8):
                    goc = ohalf * 8 + oc
                    sy.wait_ge(ev.h, ev_it3 + oc + 1)
                    sy.dma_start(out=res_out[goc * 128:(goc + 1) * 128, tb * 512:(tb + 1) * 512],
                                 in_=outt[:, (oc % 4) * 512:(oc % 4 + 1) * 512]
                                 ).then_inc(sst[oc % 4].h, 16)
            for oc in range(8):
                sst[oc % 4].n += 16

        if DEBUG:
            @block.gpsimd
            def _(g):
                g.wait_ge(ctx.sem("bar").h, ctx.sem("bar").n)
                g.dma_start(out=dbg_C[:, :], in_=CrowsD[:, :]).then_inc(st.h, 16)
                g.dma_start(out=dbg_crep[:, :], in_=crepall[:]).then_inc(st.h, 16)
            st.n += 32

        @block.sync
        def _(sy):
            st.wait(sy)
            for sem in sst:
                sem.wait(sy)
            for sem in hst:
                sem.wait(sy)
    return nc


def run_l1ffn(shards, xhat, lnrows, inputs, D_bf):
    nc = build_l1ffn()
    cst = consts_for_core()
    cst2 = attn_consts()
    wr9 = np.concatenate([np.asarray(inputs["L1_ffn_Wr"], np.float32),
                          np.asarray(inputs["L1_ffn_Wsg"], np.float32)], axis=1)
    Wg = np.asarray(inputs["L1_ffn_Wg"], np.float32)
    Wu = np.asarray(inputs["L1_ffn_Wu"], np.float32)
    Sg = np.asarray(inputs["L1_ffn_Sg"], np.float32)
    Su = np.asarray(inputs["L1_ffn_Su"], np.float32)
    G_all = np.concatenate([Wg.transpose(1, 0, 2).reshape(D, 8 * 1408), Sg], axis=1)
    U_all = np.concatenate([Wu.transpose(1, 0, 2).reshape(D, 8 * 1408), Su], axis=1)

    def tile_gu(A):
        return np.ascontiguousarray(
            A.reshape(16, 128, NKH, 128).transpose(2, 1, 0, 3).reshape(NKH, 128, D))

    G_tiles = tile_gu(G_all)
    U_tiles = tile_gu(U_all)
    in_maps = []
    for c in range(NCORES):
        m = {"res": shards[c], "xhat": xhat[c], "lnrows": lnrows[c], "wr9": wr9,
             "G_t": G_tiles, "U_t": U_tiles, "D_bf": D_bf}
        m.update(cst)
        m["c_ident_f"] = cst2["c_ident_f"]
        in_maps.append(m)
    r = run_bass_kernel_spmd(nc, in_maps, core_ids=list(range(NCORES)))
    return [r.results[c]["res_out"] for c in range(NCORES)]


def make_D_all(inputs):
    Wd = np.asarray(inputs["L1_ffn_Wd"], np.float32)
    Sd = np.asarray(inputs["L1_ffn_Sd"], np.float32)
    return np.ascontiguousarray(np.concatenate([Wd.reshape(8 * 1408, D), Sd], axis=0))



F32EPS = float(np.finfo(np.float32).eps)
ASCALE = float(1.0 / np.sqrt(64.0))

def l1a_consts():
    q = np.arange(64)[:, None]
    c = np.arange(80)[None, :]
    vis_win = (c[:, 0:64] + 16) <= q
    vis_pm = c[:, 64:80] - 64 <= q
    mask = np.concatenate([np.where(vis_win, 0.0, -1e30),
                           np.where(vis_pm, 0.0, -1e30)], axis=1).astype(np.float32)
    return {"c_wmask": np.concatenate([mask, mask], axis=0)}


def tile_strips(A, nstrip):
    Dm, F = A.shape
    return np.ascontiguousarray(
        A.reshape(NK, 128, nstrip, 128).transpose(2, 1, 0, 3).reshape(nstrip, 128, Dm))


def build_l1attna():
    nc = bass.Bass("TRN2", target_bir_lowering=False)
    res = nc.dram_tensor("res", [D, T], F32, kind="ExternalInput").ap()
    wqkv_t = nc.dram_tensor("wqkv_t", [48, 128, D], F32, kind="ExternalInput").ap()
    wqkv = nc.dram_tensor("wqkv", [D, 6144], F32, kind="ExternalInput").ap()
    wnorm = nc.dram_tensor("wnorm", [128, NK], F32, kind="ExternalInput").ap()
    pmk = nc.dram_tensor("pmk", [16, 128, 16], F32, kind="ExternalInput").ap()
    pmv = nc.dram_tensor("pmv", [16, 16, 128], F32, kind="ExternalInput").ap()
    wmask = nc.dram_tensor("c_wmask", [128, 80], F32, kind="ExternalInput").ap()
    o_out = nc.dram_tensor("o_out", [D, T], BF16, kind="ExternalOutput").ap()
    qkD = nc.dram_tensor("qkD", [4096, T], BF16,
                         kind="ExternalOutput" if DEBUG else "Internal").ap()
    vD = nc.dram_tensor("vD", [T, D], BF16,
                        kind="ExternalOutput" if DEBUG else "Internal").ap()
    cst = declare_consts(nc)

    with ExitStack() as stack:
        sb = lambda *a, **k: stack.enter_context(nc.sbuf_tensor(*a, **k))
        ps = lambda *a, **k: stack.enter_context(nc.psum_tensor(*a, **k))
        xh = sb("xh", [128, NK * T], BF16)
        xt = sb("xt", [128, 4 * TB], F32)
        sq = sb("sq", [128, 2 * TB], F32)
        rows = sb("rows", [1, 2 * T], F32)
        rows_b = sb("rows_b", [1, T], BF16)
        wn = sb("wn", [128, NK], F32)
        strip = sb("strip", [128, 2 * D], BF16)
        wvs = sb("wvs", [128, NK * 512], BF16)
        qkvb = sb("qkvb", [128, 2 * 512], BF16)
        ones_bf = sb("ones_bf", [1, 128], BF16)
        ones_f = sb("ones_f", [128, 1], F32)
        identb = sb("identb", [128, 128], BF16)
        cpool = sb("cpool", [128, 4], F32)
        dmy = sb("dmy", [1, 4], F32)
        Qh = sb("Qh", [128, T], BF16)
        Kh = sb("Kh", [128, T], BF16)
        Vh = sb("Vh", [128, 16 * 128], BF16)
        pmk_sb = sb("pmk_sb", [128, 16], BF16)
        pmv_sb = sb("pmv_sb", [16, 128], BF16)
        kfull = sb("kfull", [128, 2 * 80], BF16)
        vfull = sb("vfull", [80, 2 * 128], BF16)
        psb = sb("psb", [128, 2 * 80], BF16)
        psb2 = sb("psb2", [128, 2 * 80], BF16)
        pT = sb("pT", [80, 2 * 128], BF16)
        mcol = sb("mcol", [128, 8], F32)
        obuf = sb("obuf", [128, 16 * 128], BF16)
        wmk = sb("wmk", [128, 80], F32)
        psQ = ps("psQ", [128, 2 * 512], F32)
        psS = ps("psS", [128, 2 * 512], F32)
        psT = ps("psT", [80, 2 * 1024], BF16)
        psV = ps("psV", [128, 2 * 512], F32)
        psA = psQ[0:1, 0:512]
        block = stack.enter_context(nc.Block())

        ctx = Ctx(nc, block, dummy=dmy)
        ld, ldg, mm, ev, ac, st = (ctx.sem(x) for x in ("ld", "ldg", "mm", "ev", "ac", "st"))
        ctx.init_cpool(cpool, [F32EPS])
        ctx.barrier(("vector", "scalar", "tensor"))

        @block.sync
        def _(sy):
            ld.dma(sy.dma_start(out=ones_bf[:], in_=cst["c_ones_bf_1x128"][:, :]))
            ld.dma(sy.dma_start(out=ones_f[:], in_=cst["c_ones_f_128x1"][:, :]))
            ld.dma(sy.dma_start(out=identb[:], in_=cst["c_ident_bf"][:, :]))
            ld.dma(sy.dma_start(out=wmk[:], in_=wmask[:, :]))

        @block.gpsimd
        def _(g):
            ldg.dma(g.dma_start(out=wn[:], in_=wnorm[:, :]))
            for k in range(NK):
                ldg.dma(g.dma_start(out=wvs[:, k * 512:(k + 1) * 512],
                                    in_=wqkv[k * 128:(k + 1) * 128, 4096:4608]))

        ctx.barrier()

        NTILE = NTB * NK
        mm0, ev0, ac0 = mm.n, ev.n, ac.n
        lds = ctx.slot_sems("rmsld", 4)
        lds0 = [x.n for x in lds]

        @block.sync
        def _(sy):
            for i in range(NTILE):
                tb, k = divmod(i, NK)
                if i >= 4:
                    sy.wait_ge(ac.h, ac0 + (i - 4) + 1)
                sy.dma_start(out=xt[:, (i % 4) * TB:(i % 4 + 1) * TB],
                             in_=res[k * 128:(k + 1) * 128, tb * TB:(tb + 1) * TB]
                             ).then_inc(lds[i % 4].h, 16)
                lds[i % 4].n += 16

        @block.scalar
        def _(s):
            for i in range(NTILE):
                s.wait_ge(lds[i % 4].h, lds0[i % 4] + 16 * (i // 4 + 1))
                if i >= 2:
                    s.wait_ge(mm.h, mm0 + (i - 2) + 1)
                ac.inc(s.activation(sq[:, (i % 2) * TB:(i % 2 + 1) * TB],
                                    xt[:, (i % 4) * TB:(i % 4 + 1) * TB], AF.Square))

        @block.tensor
        def _(t):
            for i in range(NTILE):
                tb, k = divmod(i, NK)
                if k == 0 and tb > 0:
                    t.wait_ge(ev.h, ev0 + tb)
                t.wait_ge(ac.h, ac0 + i + 1)
                mm.inc(t.matmul(psA, lhsT=ones_f[:], rhs=sq[:, (i % 2) * TB:(i % 2 + 1) * TB],
                                start=(k == 0), stop=(k == NK - 1), skip_group_check=True))

        @block.vector
        def _(v):
            for tb in range(NTB):
                v.wait_ge(mm.h, mm0 + (tb + 1) * NK)
                ev.inc(v.tensor_copy(rows[:, tb * TB:(tb + 1) * TB], psA))

        ctx.barrier()

        @block.scalar
        def _(s):
            s.activation(rows[:, T:2 * T], rows[:, 0:T], AF.Ln, bias=F32EPS, scale=1.0 / D)
            s.activation(rows[:, T:2 * T], rows[:, T:2 * T], AF.Exp, scale=-0.5)
            ac.inc(s.copy(rows_b[:, :], rows[:, T:2 * T]))

        ctx.barrier()

        xal = ctx.slot_sems("xal", 4)
        ev_ap0 = ev.n
        for tb in range(NTB):
            @block.sync
            def _(sy, tb=tb):
                for k in range(NK):
                    i = tb * NK + k
                    if i >= 4:
                        sy.wait_ge(ev.h, ev_ap0 + (i - 4) + 1 + (tb if False else 0))
                    sy.dma_start(out=xt[:, (k % 4) * TB:(k % 4 + 1) * TB],
                                 in_=res[k * 128:(k + 1) * 128, tb * TB:(tb + 1) * TB]
                                 ).then_inc(xal[k % 4].h, 16)
                    xal[k % 4].n += 16

            mm1 = mm.n

            @block.tensor
            def _(t, tb=tb):
                if tb > 0:
                    t.wait_ge(ev.h, ev.n)
                mm.inc(t.matmul(psQ[:, 0:512], lhsT=ones_bf[:],
                                rhs=rows_b[:, tb * TB:(tb + 1) * TB],
                                start=True, stop=True, skip_group_check=True))

            @block.vector
            def _(v, tb=tb, mm1=mm1):
                v.wait_ge(mm.h, mm1 + 1)
                for k in range(NK):
                    i = tb * NK + k
                    v.wait_ge(xal[k % 4].h, 16 * (i // 4 + 1))
                    ev.inc(v.scalar_tensor_tensor(xh[:, k * T + tb * TB: k * T + (tb + 1) * TB],
                                                  xt[:, (k % 4) * TB:(k % 4 + 1) * TB],
                                                  wn[:, k:k + 1], psQ[:, 0:512],
                                                  ALU.mult, ALU.mult))
        ctx.barrier()

        ssl = ctx.slot_sems("ssl", 2)
        qst = ctx.slot_sems("qst", 2)
        for oc in range(32):
            slot = oc % 2

            @block.gpsimd
            def _(g, oc=oc, slot=slot):
                if oc >= 2:
                    g.wait_ge(mm.h, mm.n)
                g.dma_start(out=strip[:, slot * D:(slot + 1) * D],
                            in_=wqkv_t[oc, :, :]).then_inc(ssl[slot].h, 16)
                ssl[slot].n += 16

            for tb in range(NTB):
                it = oc * NTB + tb
                pslot = it % 2
                mmq = mm.n

                @block.tensor
                def _(t, oc=oc, tb=tb, slot=slot, pslot=pslot, it=it):
                    t.wait_ge(ssl[slot].h, ssl[slot].n)
                    if it >= 2:
                        t.wait_ge(ac.h, ac.n - 2 + 1)
                    for k in range(NK):
                        mm.inc(t.matmul(psQ[:, pslot * 512:(pslot + 1) * 512],
                                        lhsT=strip[:, slot * D + k * 128: slot * D + (k + 1) * 128],
                                        rhs=xh[:, k * T + tb * TB: k * T + (tb + 1) * TB],
                                        start=(k == 0), stop=(k == NK - 1), skip_group_check=True))

                @block.scalar
                def _(s, oc=oc, tb=tb, pslot=pslot, it=it, mmq=mmq):
                    s.wait_ge(mm.h, mmq + NK)
                    if it >= 2:
                        s.wait_ge(qst[pslot].h, 16 * (it // 2))
                    ac.inc(s.copy(qkvb[:, pslot * 512:(pslot + 1) * 512],
                                  psQ[:, pslot * 512:(pslot + 1) * 512]))

                @block.sync
                def _(sy, oc=oc, tb=tb, pslot=pslot):
                    ac.wait(sy)
                    sy.dma_start(out=qkD[oc * 128:(oc + 1) * 128, tb * TB:(tb + 1) * TB],
                                 in_=qkvb[:, pslot * 512:(pslot + 1) * 512]
                                 ).then_inc(qst[pslot].h, 16)
                qst[pslot].n += 16

        for vg in range(4):
            @block.gpsimd
            def _(g, vg=vg):
                if vg > 0:
                    g.wait_ge(mm.h, mm.n)
                for k in range(NK):
                    g.dma_start(out=wvs[:, k * 512:(k + 1) * 512],
                                in_=wqkv[k * 128:(k + 1) * 128, 4096 + vg * 512: 4096 + (vg + 1) * 512]
                                ).then_inc(ldg.h, 16)
            ldg.n += NK * 16

            for tt in range(16):
                it = vg * 16 + tt
                pslot = it % 2
                mmv = mm.n

                @block.tensor
                def _(t, vg=vg, tt=tt, pslot=pslot, it=it):
                    ldg.wait(t)
                    if it >= 2:
                        t.wait_ge(ac.h, ac.n - 2 + 1)
                    for k in range(NK):
                        mm.inc(t.matmul(psQ[:, pslot * 512:(pslot + 1) * 512],
                                        lhsT=xh[:, k * T + tt * 128: k * T + tt * 128 + 128],
                                        rhs=wvs[:, k * 512:(k + 1) * 512],
                                        start=(k == 0), stop=(k == NK - 1), skip_group_check=True))

                @block.scalar
                def _(s, pslot=pslot, it=it, mmv=mmv):
                    s.wait_ge(mm.h, mmv + NK)
                    if it >= 2:
                        s.wait_ge(qst[pslot].h, qst[pslot].n - 16)
                    ac.inc(s.copy(qkvb[:, pslot * 512:(pslot + 1) * 512],
                                  psQ[:, pslot * 512:(pslot + 1) * 512]))

                @block.sync
                def _(sy, vg=vg, tt=tt, pslot=pslot):
                    ac.wait(sy)
                    sy.dma_start(out=vD[tt * 128:(tt + 1) * 128, vg * 512:(vg + 1) * 512],
                                 in_=qkvb[:, pslot * 512:(pslot + 1) * 512]
                                 ).then_inc(qst[pslot].h, 16)
                qst[pslot].n += 16

        ctx.barrier()

        hpl = ctx.slot_sems("hpl", 1)
        ost = ctx.slot_sems("ost", 1)
        for hp in range(16):
            @block.sync
            def _(sy, hp=hp):
                if hp > 0:
                    sy.wait_ge(ost[0].h, ost[0].n)
                    sy.wait_ge(mm.h, mm.n)
                sy.dma_start(out=Qh[:], in_=qkD[hp * 128:(hp + 1) * 128, :]).then_inc(hpl[0].h, 16)
                sy.dma_start(out=Kh[:], in_=qkD[2048 + hp * 128: 2048 + (hp + 1) * 128, :]).then_inc(hpl[0].h, 16)
                for qt in range(4):
                    sy.dma_start(out=Vh[:, qt * 512:(qt + 1) * 512]
                                 .rearrange("p (tt c) -> p tt c", tt=4),
                                 in_=vD[qt * 512:(qt + 1) * 512, hp * 128:(hp + 1) * 128]
                                 .rearrange("(tt p) c -> p tt c", p=128)).then_inc(hpl[0].h, 16)
                hpl[0].n += 6 * 16

            @block.gpsimd
            def _(g, hp=hp):
                if hp > 0:
                    g.wait_ge(mm.h, mm.n)
                g.dma_start(out=pmk_sb[:], in_=pmk[hp, :, :]).then_inc(hpl[0].h, 16)
                g.dma_start(out=pmv_sb[:], in_=pmv[hp, :, :]).then_inc(hpl[0].h, 16)
            hpl[0].n += 2 * 16

            for w in range(32):
                aslot = w % 2
                hb = hpl[0].n

                @block.vector
                def _(v, hp=hp, w=w, aslot=aslot, hb=hb):
                    v.wait_ge(hpl[0].h, hb)
                    if w >= 2:
                        v.wait_ge(mm.h, mm.n)
                    ksl = slice(aslot * 80, aslot * 80 + 80)
                    v.tensor_copy(kfull[:, aslot * 80: aslot * 80 + 64], Kh[:, w * 64:(w + 1) * 64])
                    v.tensor_copy(kfull[:, aslot * 80 + 64: aslot * 80 + 80], pmk_sb[:, :])
                    v.tensor_copy(vfull[0:64, aslot * 128:(aslot + 1) * 128],
                                  Vh[(w % 2) * 64:(w % 2) * 64 + 64, (w // 2) * 128:(w // 2) * 128 + 128])
                    ev.inc(v.tensor_copy(vfull[64:80, aslot * 128:(aslot + 1) * 128], pmv_sb[:, :]))
                evk = ev.n

                @block.tensor
                def _(t, hp=hp, w=w, aslot=aslot, evk=evk):
                    t.wait_ge(ev.h, evk)
                    if w >= 2:
                        t.wait_ge(ev.h, ev.n)
                    for hh in range(2):
                        mm.inc(t.matmul(psS[hh * 64: hh * 64 + 64, aslot * 512: aslot * 512 + 80],
                                        lhsT=Qh[hh * 64:(hh + 1) * 64, w * 64:(w + 1) * 64],
                                        rhs=kfull[hh * 64:(hh + 1) * 64, aslot * 80:(aslot + 1) * 80],
                                        start=True, stop=True, skip_group_check=True))
                mmS = mm.n

                @block.vector
                def _(v, w=w, aslot=aslot, mmS=mmS):
                    v.wait_ge(mm.h, mmS)
                    v.tensor_tensor(psS[:, aslot * 512: aslot * 512 + 80],
                                    psS[:, aslot * 512: aslot * 512 + 80],
                                    wmk[:, :], ALU.add)
                    v.reduce_max(mcol[:, aslot * 4: aslot * 4 + 1],
                                 psS[:, aslot * 512: aslot * 512 + 80], axis=AX.X)
                    ev.inc(v.tensor_scalar(mcol[:, aslot * 4 + 1: aslot * 4 + 2],
                                           mcol[:, aslot * 4: aslot * 4 + 1],
                                           -ASCALE, None, ALU.mult))
                evm = ev.n

                @block.scalar
                def _(s, w=w, aslot=aslot, evm=evm):
                    s.wait_ge(ev.h, evm)
                    ac.inc(s.activation(psb[:, aslot * 80:(aslot + 1) * 80],
                                        psS[:, aslot * 512: aslot * 512 + 80], AF.Exp,
                                        scale=ASCALE, bias=mcol[:, aslot * 4 + 1: aslot * 4 + 2],
                                        accum_out=mcol[:, aslot * 4 + 2: aslot * 4 + 3]))
                acP = ac.n

                @block.scalar
                def _(s, w=w, aslot=aslot, acP=acP):
                    s.activation(mcol[:, aslot * 4 + 3: aslot * 4 + 4],
                                 mcol[:, aslot * 4 + 2: aslot * 4 + 3], AF.Ln)
                    ac.inc(s.activation(mcol[:, aslot * 4 + 3: aslot * 4 + 4],
                                        mcol[:, aslot * 4 + 3: aslot * 4 + 4], AF.Exp, scale=-1.0))
                acR = ac.n

                @block.vector
                def _(v, w=w, aslot=aslot, acR=acR):
                    v.wait_ge(ac.h, acR)
                    ev.inc(v.tensor_scalar(psb2[:, aslot * 80:(aslot + 1) * 80],
                                           psb[:, aslot * 80:(aslot + 1) * 80],
                                           mcol[:, aslot * 4 + 3: aslot * 4 + 4], None, ALU.mult))
                evn = ev.n

                @block.tensor
                def _(t, w=w, aslot=aslot, evn=evn):
                    t.wait_ge(ev.h, evn)
                    if w >= 2:
                        t.wait_ge(ac.h, ac.n)
                    mm.inc(t.transpose(psT[:, aslot * 1024: aslot * 1024 + 128],
                                       psb2[:, aslot * 80:(aslot + 1) * 80], identb[:]))
                mmT2 = mm.n

                @block.scalar
                def _(s, w=w, aslot=aslot, mmT2=mmT2):
                    s.wait_ge(mm.h, mmT2)
                    ac.inc(s.copy(pT[:, aslot * 128:(aslot + 1) * 128],
                                  psT[:, aslot * 1024: aslot * 1024 + 128]))
                acT2 = ac.n

                @block.tensor
                def _(t, w=w, aslot=aslot, acT2=acT2):
                    t.wait_ge(ac.h, acT2)
                    if w >= 2:
                        t.wait_ge(ev.h, ev.n)
                    for hh in range(2):
                        mm.inc(t.matmul(psV[hh * 64: hh * 64 + 64, aslot * 512: aslot * 512 + 64],
                                        lhsT=vfull[:, aslot * 128 + hh * 64: aslot * 128 + (hh + 1) * 64],
                                        rhs=pT[:, aslot * 128 + hh * 64: aslot * 128 + (hh + 1) * 64],
                                        start=True, stop=True, skip_group_check=True))
                mmV = mm.n

                @block.vector
                def _(v, w=w, aslot=aslot, mmV=mmV):
                    v.wait_ge(mm.h, mmV)
                    ev.inc(v.tensor_copy(obuf[:, w * 64:(w + 1) * 64],
                                         psV[:, aslot * 512: aslot * 512 + 64]))

            @block.sync
            def _(sy, hp=hp):
                ev.wait(sy)
                sy.dma_start(out=o_out[hp * 128:(hp + 1) * 128, :], in_=obuf[:]).then_inc(ost[0].h, 16)
            ost[0].n += 16

        @block.sync
        def _(sy):
            ost[0].wait(sy)
            for sem in qst:
                sem.wait(sy)
    return nc


def run_l1attna(shards, inputs):
    nc = build_l1attna()
    cst = consts_for_core()
    cst.update(l1a_consts())
    wqkv = np.ascontiguousarray(inputs["L1_attn_a_w_qkv"], dtype=np.float32)
    pm = np.asarray(inputs["L1_attn_a_pm"], dtype=np.float32)
    pmk_f = np.ascontiguousarray(
        pm[0].transpose(0, 2, 1).reshape(16, 128, 16))
    pmv_f = np.ascontiguousarray(
        pm[1].transpose(1, 0, 2).reshape(16, 16, 128)) if False else np.ascontiguousarray(
        pm[1].reshape(16, 2, 16, 64).transpose(0, 2, 1, 3).reshape(16, 16, 128))
    wn = np.ascontiguousarray(np.asarray(inputs["L1_attn_a_w_norm"], np.float32).reshape(NK, 128).T)
    in_maps = []
    for c in range(NCORES):
        m = {"res": shards[c], "wqkv_t": tile_strips(wqkv, 48), "wqkv": wqkv,
             "wnorm": wn, "pmk": pmk_f, "pmv": pmv_f}
        m.update(cst)
        in_maps.append(m)
    r = run_bass_kernel_spmd(nc, in_maps, core_ids=list(range(NCORES)))
    o = [r.results[c]["o_out"] for c in range(NCORES)]
    wout = np.ascontiguousarray(inputs["L1_attn_a_w_out"], dtype=np.float32)
    return run_addmm(shards, o, wout)


def run_l1attna_o(xln, inputs):
    nc = build_l1attna()
    cst = consts_for_core()
    cst.update(l1a_consts())
    wqkv = np.ascontiguousarray(inputs["L1_attn_a_w_qkv"], dtype=np.float32)
    pm = np.asarray(inputs["L1_attn_a_pm"], dtype=np.float32)
    pmk_f = np.ascontiguousarray(pm[0].transpose(0, 2, 1).reshape(16, 128, 16))
    pmv_f = np.ascontiguousarray(
        pm[1].reshape(16, 2, 16, 64).transpose(0, 2, 1, 3).reshape(16, 16, 128))
    wn = np.ascontiguousarray(np.asarray(inputs["L1_attn_a_w_norm"], np.float32).reshape(NK, 128).T)
    in_maps = []
    for c in range(NCORES):
        m = {"res": xln[c], "wqkv_t": tile_strips(wqkv, 48), "wqkv": wqkv,
             "wnorm": wn, "pmk": pmk_f, "pmv": pmv_f}
        m.update(cst)
        in_maps.append(m)
    r = run_bass_kernel_spmd(nc, in_maps, core_ids=list(range(NCORES)))
    return [r.results[c]["o_out"] for c in range(NCORES)]



def l1b_consts():
    h = np.arange(16)[:, None]
    g = np.arange(16)[None, :]
    return {"c_hmask": np.where(g <= h, 0.0, -1e30).astype(np.float32)}


def build_l1attnb():
    nc = bass.Bass("TRN2", target_bir_lowering=False)
    xhat = nc.dram_tensor("xhat", [D, T], BF16, kind="ExternalInput").ap()
    res = nc.dram_tensor("res", [D, T], F32, kind="ExternalInput").ap()
    wq_t = nc.dram_tensor("wq_t", [16, 128, D], F32, kind="ExternalInput").ap()
    wk_t = nc.dram_tensor("wk_t", [16, 128, D], F32, kind="ExternalInput").ap()
    wv = nc.dram_tensor("wv", [D, D], F32, kind="ExternalInput").ap()
    hmask = nc.dram_tensor("c_hmask", [16, 16], F32, kind="ExternalInput").ap()
    res_out = nc.dram_tensor("res_out", [D, T], F32, kind="ExternalOutput").ap()
    ik = "ExternalOutput" if L1B_DEBUG else "Internal"
    qkD = nc.dram_tensor("qkD", [4096, T], BF16, kind=ik).ap()
    vD = nc.dram_tensor("vD", [T, D], BF16, kind=ik).ap()
    cst = declare_consts(nc)

    with ExitStack() as stack:
        sb = lambda *a, **k: stack.enter_context(nc.sbuf_tensor(*a, **k))
        ps = lambda *a, **k: stack.enter_context(nc.psum_tensor(*a, **k))
        xh = sb("xh", [128, NK * T], BF16)
        strip = sb("strip", [128, 2 * D], BF16)
        wvs = sb("wvs", [128, NK * 512], BF16)
        qkvb = sb("qkvb", [128, 2 * 512], BF16)
        qb = sb("qb", [128, 16 * 512], BF16)
        kb = sb("kb", [128, 16 * 512], BF16)
        vstage = sb("vstage", [16, 2 * 32 * 128], BF16)
        Ppad = sb("Ppad", [16, 2 * 32 * 128], BF16)
        PTsb = sb("PTsb", [128, 2 * 32 * 16], BF16)
        rsb = sb("rsb", [128, 2 * 512], F32)
        osum = sb("osum", [128, 2 * 512], F32)
        Smb = sb("Smb", [16, 2 * 512], F32)
        dmb = sb("dmb", [16, 2 * 512], F32)
        exb = sb("exb", [16, 2 * 512], F32)
        hmk = sb("hmk", [16, 16], F32)
        rowm = sb("rowm", [16, 2 * 32], F32)
        rowl = sb("rowl", [16, 2 * 32], F32)
        rowr = sb("rowr", [16, 2 * 32], F32)
        identb = sb("identb", [128, 128], BF16)
        dmy = sb("dmy", [1, 4], F32)
        psQ = ps("psQ", [128, 2 * 512], F32)
        psT2 = ps("psT2", [128, 1024], BF16)
        psV = ps("psV", [128, 2 * 512], F32)
        psS = psQ[0:16, 0:1024]
        block = stack.enter_context(nc.Block())

        ctx = Ctx(nc, block, dummy=dmy)
        ld, ldg, mm, ev, ac, st = (ctx.sem(x) for x in ("ld", "ldg", "mm", "ev", "ac", "st"))

        @block.sync
        def _(sy):
            ld.dma(sy.dma_start(out=identb[:], in_=cst["c_ident_bf"][:, :]))
            ld.dma(sy.dma_start(out=hmk[:], in_=hmask[:, :]))
            for k in range(NK):
                for tb4 in range(4):
                    ld.dma(sy.dma_start(out=xh[:, k * T + tb4 * 512: k * T + (tb4 + 1) * 512],
                                        in_=xhat[k * 128:(k + 1) * 128, tb4 * 512:(tb4 + 1) * 512]))
        ctx.barrier()

        ssl = ctx.slot_sems("ssl", 2)
        qst = ctx.slot_sems("qst", 2)
        for oc in range(32):
            slot = oc % 2

            @block.gpsimd
            def _(g, oc=oc, slot=slot):
                if oc >= 2:
                    g.wait_ge(mm.h, mm.n)
                src = wq_t[oc, :, :] if oc < 16 else wk_t[oc - 16, :, :]
                g.dma_start(out=strip[:, slot * D:(slot + 1) * D], in_=src).then_inc(ssl[slot].h, 16)
                ssl[slot].n += 16

            for tb in range(NTB):
                it = oc * NTB + tb
                pslot = it % 2
                mmq = mm.n

                @block.tensor
                def _(t, oc=oc, tb=tb, slot=slot, pslot=pslot, it=it):
                    t.wait_ge(ssl[slot].h, ssl[slot].n)
                    if it >= 2:
                        t.wait_ge(ac.h, ac.n - 1)
                    for k in range(NK):
                        mm.inc(t.matmul(psQ[:, pslot * 512:(pslot + 1) * 512],
                                        lhsT=strip[:, slot * D + k * 128: slot * D + (k + 1) * 128],
                                        rhs=xh[:, k * T + tb * TB: k * T + (tb + 1) * TB],
                                        start=(k == 0), stop=(k == NK - 1), skip_group_check=True))

                @block.scalar
                def _(s, pslot=pslot, it=it, mmq=mmq):
                    s.wait_ge(mm.h, mmq + NK)
                    if it >= 2:
                        s.wait_ge(qst[pslot].h, 16 * (it // 2))
                    ac.inc(s.copy(qkvb[:, pslot * 512:(pslot + 1) * 512],
                                  psQ[:, pslot * 512:(pslot + 1) * 512]))

                @block.sync
                def _(sy, oc=oc, tb=tb, pslot=pslot):
                    ac.wait(sy)
                    sy.dma_start(out=qkD[oc * 128:(oc + 1) * 128, tb * TB:(tb + 1) * TB],
                                 in_=qkvb[:, pslot * 512:(pslot + 1) * 512]
                                 ).then_inc(qst[pslot].h, 16)
                qst[pslot].n += 16

        for vg in range(4):
            @block.gpsimd
            def _(g, vg=vg):
                g.wait_ge(mm.h, mm.n)
                for k in range(NK):
                    g.dma_start(out=wvs[:, k * 512:(k + 1) * 512],
                                in_=wv[k * 128:(k + 1) * 128, vg * 512:(vg + 1) * 512]
                                ).then_inc(ldg.h, 16)
            ldg.n += NK * 16

            for tt in range(16):
                it = vg * 16 + tt
                pslot = it % 2
                mmv = mm.n

                @block.tensor
                def _(t, tt=tt, pslot=pslot, it=it):
                    ldg.wait(t)
                    if it >= 2:
                        t.wait_ge(ac.h, ac.n - 1)
                    for k in range(NK):
                        mm.inc(t.matmul(psQ[:, pslot * 512:(pslot + 1) * 512],
                                        lhsT=xh[:, k * T + tt * 128: k * T + tt * 128 + 128],
                                        rhs=wvs[:, k * 512:(k + 1) * 512],
                                        start=(k == 0), stop=(k == NK - 1), skip_group_check=True))

                @block.scalar
                def _(s, pslot=pslot, it=it, mmv=mmv):
                    s.wait_ge(mm.h, mmv + NK)
                    s.wait_ge(qst[pslot].h, qst[pslot].n - 16)
                    ac.inc(s.copy(qkvb[:, pslot * 512:(pslot + 1) * 512],
                                  psQ[:, pslot * 512:(pslot + 1) * 512]))

                @block.sync
                def _(sy, vg=vg, tt=tt, pslot=pslot):
                    ac.wait(sy)
                    sy.dma_start(out=vD[tt * 128:(tt + 1) * 128, vg * 512:(vg + 1) * 512],
                                 in_=qkvb[:, pslot * 512:(pslot + 1) * 512]
                                 ).then_inc(qst[pslot].h, 16)
                qst[pslot].n += 16

        ctx.barrier()

        qbl = ctx.slot_sems("qbl", 1)
        vsl = ctx.slot_sems("vsl", 2)
        osl = ctx.slot_sems("osl", 2)
        rsl = ctx.slot_sems("rsl", 2)
        addEv = {0: 0, 1: 0}
        res3 = res.rearrange("(h d) t -> d h t", h=16)
        ro3 = res_out.rearrange("(h d) t -> d h t", h=16)
        hv = lambda tile, sl: tile[:, sl * 512:(sl + 1) * 512].rearrange("p (h t) -> p h t", h=16)
        for tblk in range(4):
            @block.sync
            def _(sy, tblk=tblk):
                if tblk > 0:
                    sy.wait_ge(mm.h, mm.n)
                for c in range(16):
                    sy.dma_start(out=qb[:, c * 512:(c + 1) * 512],
                                 in_=qkD[c * 128:(c + 1) * 128, tblk * 512:(tblk + 1) * 512]
                                 ).then_inc(qbl[0].h, 16)
                    sy.dma_start(out=kb[:, c * 512:(c + 1) * 512],
                                 in_=qkD[2048 + c * 128: 2048 + (c + 1) * 128, tblk * 512:(tblk + 1) * 512]
                                 ).then_inc(qbl[0].h, 16)
                qbl[0].n += 32 * 16

            if L1B_STAGE < 2:
                continue
            for pk in range(16):
                gpk = tblk * 16 + pk
                aslot = pk % 2
                t0g = tblk * 512 + pk * 32

                @block.gpsimd
                def _(g, gpk=gpk, aslot=aslot, t0g=t0g):
                    if gpk >= 2:
                        g.wait_ge(mm.h, mm.n)
                    g.dma_start(out=vstage[:, aslot * 4096:(aslot + 1) * 4096]
                                .rearrange("p (t c) -> p t c", t=32),
                                in_=vD[t0g:t0g + 32, :].rearrange("t (g c) -> g t c", c=128)
                                ).then_inc(vsl[aslot].h, 16)
                    vsl[aslot].n += 16

                qv = qb[:].rearrange("p (c w) -> p w c", w=512)
                kv = kb[:].rearrange("p (c w) -> p w c", w=512)

                @block.tensor
                def _(t, gpk=gpk, pk=pk, aslot=aslot, tblk=tblk):
                    qbl[0].wait(t)
                    if gpk >= 2:
                        t.wait_ge(ev.h, ev.n)
                    for ts in range(32):
                        tl = pk * 32 + ts
                        lq = qv[:, tl:tl + 1, :].rearrange("p a c -> p (a c)")
                        lk = kv[:, tl:tl + 1, :].rearrange("p a c -> p (a c)")
                        mm.inc(t.matmul(psQ[0:16, aslot * 512 + ts * 16: aslot * 512 + (ts + 1) * 16],
                                        lhsT=lq, rhs=lk, start=True, stop=True,
                                        skip_group_check=True))
                mmS1 = mm.n

                S3 = lambda sl: psQ[0:16, sl * 512:(sl + 1) * 512].rearrange("p (t g) -> p t g", g=16)
                v3 = lambda tile, sl: tile[:, sl * 512:(sl + 1) * 512].rearrange("p (t g) -> p t g", g=16)

                if L1B_STAGE < 1.8:
                    continue

                @block.vector
                def _(v, pk=pk, aslot=aslot, mmS1=mmS1):
                    v.wait_ge(mm.h, mmS1)
                    v.tensor_tensor(v3(Smb, aslot), S3(aslot),
                                    hmk[:].rearrange("p (a g) -> p a g", a=1)
                                    .broadcast_to((16, 32, 16)), ALU.add)
                    v.reduce_max(rowm[:, aslot * 32:(aslot + 1) * 32], v3(Smb, aslot), axis=AX.X)
                    ev.inc(v.tensor_tensor(v3(dmb, aslot), v3(Smb, aslot),
                                           rowm[:, aslot * 32:(aslot + 1) * 32]
                                           .rearrange("p (t o) -> p t o", o=1)
                                           .broadcast_to((16, 32, 16)), ALU.subtract))
                evs = ev.n

                if L1B_STAGE < 1.95:
                    continue

                @block.scalar
                def _(s, pk=pk, aslot=aslot, evs=evs):
                    s.wait_ge(ev.h, evs)
                    ac.inc(s.activation(exb[:, aslot * 512:(aslot + 1) * 512],
                                        dmb[:, aslot * 512:(aslot + 1) * 512], AF.Exp))
                acE = ac.n

                @block.vector
                def _(v, pk=pk, aslot=aslot, acE=acE):
                    v.wait_ge(ac.h, acE)
                    ev.inc(v.reduce_sum(rowl[:, aslot * 32:(aslot + 1) * 32], v3(exb, aslot), axis=AX.X))
                evl = ev.n

                @block.scalar
                def _(s, pk=pk, aslot=aslot, evl=evl):
                    s.wait_ge(ev.h, evl)
                    s.activation(rowr[:, aslot * 32:(aslot + 1) * 32],
                                 rowl[:, aslot * 32:(aslot + 1) * 32], AF.Ln)
                    ac.inc(s.activation(rowr[:, aslot * 32:(aslot + 1) * 32],
                                        rowr[:, aslot * 32:(aslot + 1) * 32], AF.Exp, scale=-1.0))
                acR = ac.n

                if L1B_STAGE < 1.99:
                    continue

                @block.vector
                def _(v, gpk=gpk, aslot=aslot, acR=acR):
                    v.wait_ge(ac.h, acR)
                    if gpk >= 2:
                        v.wait_ge(mm.h, mm.n)
                    out4 = Ppad[:, aslot * 4096:(aslot + 1) * 4096] \
                        .rearrange("p (t gq) -> p t gq", t=32)[:, :, 0:16]
                    in4 = exb[:, aslot * 512:(aslot + 1) * 512] \
                        .rearrange("p (t g) -> p t g", t=32)
                    rl4 = rowr[:, aslot * 32:(aslot + 1) * 32] \
                        .rearrange("p (t o) -> p t o", o=1).broadcast_to((16, 32, 16))
                    ev.inc(v.tensor_tensor(out4, in4, rl4, ALU.mult))
                evP = ev.n

                if L1B_STAGE < 3:
                    continue
                acPT0 = ac.n
                for pr in range(32):
                    @block.tensor
                    def _(t, gpk=gpk, aslot=aslot, pr=pr, evP=evP, acPT0=acPT0):
                        if pr == 0:
                            t.wait_ge(ev.h, evP)
                            if gpk >= 1:
                                t.wait_ge(ac.h, acPT0)
                        if pr >= 1:
                            t.wait_ge(ac.h, acPT0 + pr)
                        mm.inc(t.transpose(psT2[:, 0:16],
                                           Ppad[:, aslot * 4096 + pr * 128: aslot * 4096 + (pr + 1) * 128],
                                           identb[0:16, 0:16]))
                    mmT = mm.n

                    @block.scalar
                    def _(s, pk=pk, aslot=aslot, pr=pr, mmT=mmT):
                        s.wait_ge(mm.h, mmT)
                        ac.inc(s.copy(PTsb[:, aslot * 512 + pr * 16: aslot * 512 + (pr + 1) * 16],
                                      psT2[0:128, 0:16]))
                acPT = ac.n

                if L1B_STAGE < 2.8:
                    if L1B_PACKBAR:
                        ctx.barrier()
                    continue

                mmAV0 = mm.n

                @block.gpsimd
                def _(g, gpk=gpk, aslot=aslot, t0g=t0g, _prev=addEv[aslot]):
                    if _prev > 0:
                        g.wait_ge(ev.h, _prev)
                    g.dma_start(out=hv(rsb, aslot),
                                in_=res3[:, :, t0g:t0g + 32]).then_inc(rsl[aslot].h, 16)
                    rsl[aslot].n += 16

                @block.tensor
                def _(t, gpk=gpk, aslot=aslot, acPT=acPT, _prev=addEv[aslot]):
                    t.wait_ge(ac.h, acPT)
                    t.wait_ge(vsl[aslot].h, vsl[aslot].n)
                    if _prev > 0:
                        t.wait_ge(ev.h, _prev)
                    for ts in range(32):
                        mm.inc(t.matmul(psV[:, aslot * 512 + ts * 16: aslot * 512 + (ts + 1) * 16],
                                        lhsT=vstage[:, aslot * 4096 + ts * 128:(ts + 1) * 128 + aslot * 4096],
                                        rhs=PTsb[0:16, aslot * 512 + ts * 16: aslot * 512 + (ts + 1) * 16],
                                        start=True, stop=True, skip_group_check=True))
                mmAV = mm.n

                @block.vector
                def _(v, aslot=aslot, mmAV=mmAV, _rn=rsl[aslot].n, _on=osl[aslot].n):
                    v.wait_ge(mm.h, mmAV)
                    v.wait_ge(rsl[aslot].h, _rn)
                    if _on > 0:
                        v.wait_ge(osl[aslot].h, _on)
                    ev.inc(v.tensor_tensor(
                        hv(osum, aslot), hv(rsb, aslot),
                        psV[:, aslot * 512:(aslot + 1) * 512].rearrange("p (t h) -> p h t", t=32),
                        ALU.add))
                addEv[aslot] = ev.n

                @block.sync
                def _(sy, aslot=aslot, t0g=t0g, _ev=addEv[aslot]):
                    sy.wait_ge(ev.h, _ev)
                    sy.dma_start(out=ro3[:, :, t0g:t0g + 32],
                                 in_=hv(osum, aslot)).then_inc(osl[aslot].h, 16)
                osl[aslot].n += 16
                if L1B_PACKBAR:
                    ctx.barrier()

        @block.sync
        def _(sy):
            for sem in osl:
                sem.wait(sy)
            for sem in qst:
                sem.wait(sy)
    return nc


def build_addT():
    nc = bass.Bass("TRN2", target_bir_lowering=False)
    res = nc.dram_tensor("res", [D, T], F32, kind="ExternalInput").ap()
    oD = nc.dram_tensor("oD", [T, D], BF16, kind="ExternalInput").ap()
    res_out = nc.dram_tensor("res_out", [D, T], F32, kind="ExternalOutput").ap()

    with ExitStack() as stack:
        sb = lambda *a, **k: stack.enter_context(nc.sbuf_tensor(*a, **k))
        ot = sb("ot", [128, 4 * 128], BF16)
        of = sb("of", [128, 4 * 128], BF16)
        xt = sb("xt", [128, 4 * 128], F32)
        outt = sb("outt", [128, 4 * 128], F32)
        block = stack.enter_context(nc.Block())
        ctx = Ctx(nc, block)
        ev = ctx.sem("ev")
        lsl = ctx.slot_sems("lsl", 4)
        tsl = ctx.slot_sems("tsl", 4)
        rsl = ctx.slot_sems("rsl", 4)
        ssl = ctx.slot_sems("ssl", 4)
        NTT = 256

        @block.gpsimd
        def _(g):
            for i in range(NTT):
                oc, tt = divmod(i, 16)
                sl = i % 4
                if i >= 4:
                    g.wait_ge(tsl[sl].h, 16 * (i // 4))
                g.dma_start(out=ot[:, sl * 128:(sl + 1) * 128],
                            in_=oD[tt * 128:(tt + 1) * 128, oc * 128:(oc + 1) * 128]
                            ).then_inc(lsl[sl].h, 16)
                lsl[sl].n += 16

        @block.sync
        def _(sy):
            for i in range(NTT):
                oc, tt = divmod(i, 16)
                sl = i % 4
                sy.wait_ge(lsl[sl].h, 16 * (i // 4 + 1))
                if i >= 4:
                    sy.wait_ge(ev.h, i - 4 + 1)
                sy.dma_start(out=of[:, sl * 128:(sl + 1) * 128],
                             in_=ot[:, sl * 128:(sl + 1) * 128], transpose=True
                             ).then_inc(tsl[sl].h, 16)
                tsl[sl].n += 16

        @block.scalar
        def _(s):
            for i in range(NTT):
                oc, tt = divmod(i, 16)
                sl = i % 4
                if i >= 4:
                    s.wait_ge(ev.h, i - 4 + 1)
                s.dma_start(out=xt[:, sl * 128:(sl + 1) * 128],
                            in_=res[oc * 128:(oc + 1) * 128, tt * 128:(tt + 1) * 128]
                            ).then_inc(rsl[sl].h, 16)
                rsl[sl].n += 16

        @block.vector
        def _(v):
            for i in range(NTT):
                sl = i % 4
                v.wait_ge(tsl[sl].h, 16 * (i // 4 + 1))
                v.wait_ge(rsl[sl].h, 16 * (i // 4 + 1))
                if i >= 4:
                    v.wait_ge(ssl[sl].h, 16 * (i // 4))
                ev.inc(v.tensor_tensor(outt[:, sl * 128:(sl + 1) * 128],
                                       xt[:, sl * 128:(sl + 1) * 128],
                                       of[:, sl * 128:(sl + 1) * 128], ALU.add))

        @block.gpsimd
        def _(g):
            for i in range(NTT):
                oc, tt = divmod(i, 16)
                sl = i % 4
                g.wait_ge(ev.h, i + 1)
                g.dma_start(out=res_out[oc * 128:(oc + 1) * 128, tt * 128:(tt + 1) * 128],
                            in_=outt[:, sl * 128:(sl + 1) * 128]).then_inc(ssl[sl].h, 16)
                ssl[sl].n += 16

        @block.sync
        def _(sy):
            for sem in ssl:
                sem.wait(sy)
    return nc


def run_l1attnb(shards, xhat, inputs):
    nc = build_l1attnb()
    cst = consts_for_core()
    cst.update(l1b_consts())
    WqT = np.ascontiguousarray(np.asarray(inputs["L1_attn_b_Wq"], np.float32).T)
    WkT = np.ascontiguousarray(np.asarray(inputs["L1_attn_b_Wk"], np.float32).T)
    WvT = np.ascontiguousarray(np.asarray(inputs["L1_attn_b_Wv"], np.float32).T)
    in_maps = []
    for c in range(NCORES):
        m = {"xhat": xhat[c], "res": shards[c], "wq_t": tile_strips(WqT, 16),
             "wk_t": tile_strips(WkT, 16), "wv": WvT}
        m.update(cst)
        in_maps.append(m)
    r = run_bass_kernel_spmd(nc, in_maps, core_ids=list(range(NCORES)))
    return [r.results[c]["res_out"] for c in range(NCORES)]



INNER = 4096
L0B_CONVCH = 4352
L0B_DINP = 8512
L0B_DPAD = 8576
L0B_NOC = 67
L0B_CL = 64
L0B_NCK = T // L0B_CL


def l0b_consts():
    i = np.arange(64)
    U = (i[:, None] <= i[None, :]).astype(np.float32)
    Ls = (i[:, None] > i[None, :]).astype(np.float32)
    E2 = np.zeros((2, 128), dtype=np.float32)
    E2[0, 0:64] = 1.0
    E2[1, 64:128] = 1.0
    return {
        "c_u64_bf": U.astype(BF),
        "c_negu64_bf": (-U).astype(BF),
        "c_lstrict_bf": Ls.astype(BF),
        "c_masku_f": U,
        "c_onescol_bf": np.ones((128, 1), dtype=BF),
        "c_e2_bf": E2.astype(BF),
    }


_L0B_CONST_SPECS = [
    ("c_u64_bf", (64, 64), BF16), ("c_negu64_bf", (64, 64), BF16),
    ("c_lstrict_bf", (64, 64), BF16), ("c_masku_f", (64, 64), F32),
    ("c_onescol_bf", (128, 1), BF16), ("c_e2_bf", (2, 128), BF16),
]


def l0b_declare(nc):
    return {n: nc.dram_tensor(n, list(s), d, kind="ExternalInput").ap()
            for n, s, d in _L0B_CONST_SPECS}


def l0b_host_prep(inputs):
    W_in = np.asarray(inputs["L0_attn_b_W_in"], np.float32)
    WTp = np.zeros((D, L0B_DPAD), np.float32)
    WTp[:, 0:L0B_DINP] = W_in.T
    conv_w = np.asarray(inputs["L0_attn_b_conv_w"], np.float32)
    conv_b = np.asarray(inputs["L0_attn_b_conv_b"], np.float32)
    return {
        "w_t": tile_strips(WTp, L0B_NOC),
        "wconv": np.ascontiguousarray(conv_w.reshape(34, 128, 4).transpose(1, 0, 2).reshape(128, 136)),
        "bconv": np.ascontiguousarray(conv_b.reshape(34, 128).T),
        "dtb": np.asarray(inputs["L0_attn_b_dt_bias"], np.float32).reshape(64, 1),
        "alog": np.asarray(inputs["L0_attn_b_A_log"], np.float32).reshape(1, 64),
        "dexp": np.ascontiguousarray(
            np.repeat(np.asarray(inputs["L0_attn_b_D_param"], np.float32), 64).reshape(32, 128).T),
        "nw": np.ascontiguousarray(
            np.asarray(inputs["L0_attn_b_norm_w"], np.float32).reshape(32, 128).T),
        "wot": np.ascontiguousarray(np.asarray(inputs["L0_attn_b_W_out"], np.float32).T),
    }


def build_l0b_proj():
    nc = bass.Bass("TRN2", target_bir_lowering=False)
    xhat = nc.dram_tensor("xhat", [D, T], BF16, kind="ExternalInput").ap()
    w_t = nc.dram_tensor("w_t", [L0B_NOC, 128, D], F32, kind="ExternalInput").ap()
    zx = nc.dram_tensor("zx", [L0B_DPAD, T], F32, kind="ExternalOutput").ap()

    with ExitStack() as stack:
        sb = lambda *a, **k: stack.enter_context(nc.sbuf_tensor(*a, **k))
        ps = lambda *a, **k: stack.enter_context(nc.psum_tensor(*a, **k))
        xh = sb("xh", [128, NK * T], BF16)
        strip = sb("strip", [128, 2 * D], BF16)
        outb = sb("outb", [128, 2 * 512], F32)
        dmy = sb("dmy", [1, 4], F32)
        psQ = ps("psQ", [128, 2 * 512], F32)
        block = stack.enter_context(nc.Block())

        ctx = Ctx(nc, block, dummy=dmy)
        ld, mm, ac = (ctx.sem(x) for x in ("ld", "mm", "ac"))
        ssl = ctx.slot_sems("ssl", 2)
        qst = ctx.slot_sems("qst", 2)

        @block.sync
        def _(sy):
            for k in range(NK):
                for tb4 in range(4):
                    ld.dma(sy.dma_start(out=xh[:, k * T + tb4 * 512: k * T + (tb4 + 1) * 512],
                                        in_=xhat[k * 128:(k + 1) * 128, tb4 * 512:(tb4 + 1) * 512]))
        ctx.barrier()

        for oc in range(L0B_NOC):
            slot = oc % 2

            @block.gpsimd
            def _(g, oc=oc, slot=slot):
                if oc >= 2:
                    g.wait_ge(mm.h, mm.n)
                g.dma_start(out=strip[:, slot * D:(slot + 1) * D],
                            in_=w_t[oc, :, :]).then_inc(ssl[slot].h, 16)
                ssl[slot].n += 16

            for tb in range(NTB):
                it = oc * NTB + tb
                pslot = it % 2
                mmq = mm.n

                @block.tensor
                def _(t, oc=oc, tb=tb, slot=slot, pslot=pslot, it=it):
                    t.wait_ge(ssl[slot].h, ssl[slot].n)
                    if it >= 2:
                        t.wait_ge(ac.h, ac.n - 1)
                    for k in range(NK):
                        mm.inc(t.matmul(psQ[:, pslot * 512:(pslot + 1) * 512],
                                        lhsT=strip[:, slot * D + k * 128: slot * D + (k + 1) * 128],
                                        rhs=xh[:, k * T + tb * TB: k * T + (tb + 1) * TB],
                                        start=(k == 0), stop=(k == NK - 1), skip_group_check=True))

                @block.scalar
                def _(s, pslot=pslot, it=it, mmq=mmq):
                    s.wait_ge(mm.h, mmq + NK)
                    if it >= 2:
                        s.wait_ge(qst[pslot].h, 16 * (it // 2))
                    ac.inc(s.copy(outb[:, pslot * 512:(pslot + 1) * 512],
                                  psQ[:, pslot * 512:(pslot + 1) * 512]))

                @block.sync
                def _(sy, oc=oc, tb=tb, pslot=pslot):
                    ac.wait(sy)
                    sy.dma_start(out=zx[oc * 128:(oc + 1) * 128, tb * TB:(tb + 1) * TB],
                                 in_=outb[:, pslot * 512:(pslot + 1) * 512]
                                 ).then_inc(qst[pslot].h, 16)
                qst[pslot].n += 16

        @block.sync
        def _(sy):
            for sem in qst:
                sem.wait(sy)
    return nc


def build_l0b_main():
    nc = bass.Bass("TRN2", target_bir_lowering=False)
    zx = nc.dram_tensor("zx", [L0B_DPAD, T], F32, kind="ExternalInput").ap()
    xprev = nc.dram_tensor("xprev", [L0B_CONVCH, 3], F32, kind="ExternalInput").ap()
    wconv = nc.dram_tensor("wconv", [128, 136], F32, kind="ExternalInput").ap()
    bconv = nc.dram_tensor("bconv", [128, 34], F32, kind="ExternalInput").ap()
    dtb = nc.dram_tensor("dtb", [64, 1], F32, kind="ExternalInput").ap()
    alog = nc.dram_tensor("alog", [1, 64], F32, kind="ExternalInput").ap()
    dexp_in = nc.dram_tensor("dexp", [128, 32], F32, kind="ExternalInput").ap()
    y_fm = nc.dram_tensor("y_fm", [INNER, T], BF16, kind="ExternalOutput").ap()
    S_loc = nc.dram_tensor("S_loc", [128, INNER], F32, kind="ExternalOutput").ap()
    atot = nc.dram_tensor("atot", [64, 1], F32, kind="ExternalOutput").ap()
    dcum = nc.dram_tensor("dcum", [64, T], F32, kind="ExternalOutput").ap()
    C_st = nc.dram_tensor("C_st", [128, T], BF16, kind="ExternalOutput").ap()
    x_fm = nc.dram_tensor("x_fm", [INNER, T], BF16,
                          kind="ExternalOutput" if L0B_DEBUG else "Internal").ap()
    cst = declare_consts(nc)
    lcst = l0b_declare(nc)

    with ExitStack() as stack:
        sb = lambda *a, **k: stack.enter_context(nc.sbuf_tensor(*a, **k))
        ps = lambda *a, **k: stack.enter_context(nc.psum_tensor(*a, **k))
        xc = sb("xc", [128, 2 * 2056], F32)
        cta = sb("cta", [128, 2048], F32)
        ctb = sb("ctb", [128, 2048], F32)
        ctc = sb("ctc", [128, 2048], F32)
        xbcb = sb("xbcb", [128, 2 * 2048], BF16)
        Bfm = sb("Bfm", [128, T], BF16)
        Cfm = sb("Cfm", [128, T], BF16)
        wcv = sb("wcv", [128, 136], F32)
        bcv = sb("bcv", [128, 34], F32)
        dtbc = sb("dtbc", [64, 1], F32)
        alr = sb("alr", [1, 64], F32)
        dxp = sb("dxp", [128, 32], F32)
        dtr = sb("dtr", [64, T], F32)
        d64a = sb("d64a", [64, T], F32)
        d64b = sb("d64b", [64, T], F32)
        dtspb = sb("dtspb", [64, T], BF16)
        erow = sb("erow", [1, 64], F32)
        negrow = sb("negrow", [1, 64], BF16)
        Arep = sb("Arep", [64, 64], BF16)
        xstage = sb("xstage", [128, 2 * 2048], BF16)
        xtl = sb("xtl", [64, 2 * 4096], BF16)
        dtT = sb("dtT", [64, 2 * 64], BF16)
        aT = sb("aT", [64, 2 * 64], BF16)
        BT = sb("BT", [64, 2 * 128], BF16)
        asrb = sb("asrb", [1, 2 * 64], BF16)
        easr = sb("easr", [1, 2 * 64], BF16)
        sdoT = sb("sdoT", [64, 2 * 64], F32)
        dsT = sb("dsT", [64, 2 * 64], BF16)
        Cm = sb("Cm", [64, 2 * 64], F32)
        Emat = sb("Emat", [128, 2 * 64], F32)
        ascol = sb("ascol", [64, 2], F32)
        pcol = sb("pcol", [64, 2], F32)
        dcs = sb("dcs", [64, 2 * 64], F32)
        Scur = [sb("Sa", [128, INNER], F32), sb("Sb", [128, INNER], F32)]
        Sbf = sb("Sbf", [128, 2 * INNER], BF16)
        yT = sb("yT", [64, 2 * 4096], BF16)
        Rb = sb("Rb", [64, 2 * 256], BF16)
        Gb = sb("Gb", [64, 2 * 256], BF16)
        Bt4 = sb("Bt4", [64, 2 * 512], BF16)
        Lsb = sb("Lsb", [64, 2 * 256], F32)
        tmpY = sb("tmpY", [64, 256], F32)
        tmpS = sb("tmpS", [128, 256], F32)
        ytmp = sb("ytmp", [128, 512], BF16)
        ystage = sb("ystage", [128, 2 * 2048], BF16)
        identb = sb("identb", [128, 128], BF16)
        u64b = sb("u64b", [64, 64], BF16)
        negu = sb("negu", [64, 64], BF16)
        lstr = sb("lstr", [64, 64], BF16)
        masku = sb("masku", [64, 64], F32)
        onec = sb("onec", [128, 1], BF16)
        ones_bf = sb("ones_bf", [1, 128], BF16)
        cpool = sb("cpool", [128, 4], F32)
        dmy = sb("dmy", [1, 4], F32)
        psT = ps("psT", [128, 1024], BF16)
        psRow = ps("psRow", [128, 512], F32)
        psD = [ps("psDa", [64, 512], F32), ps("psDb", [64, 512], F32)]
        psY = [ps("psYa", [64, 512], F32), ps("psYb", [64, 512], F32)]
        psS = [ps("psSa", [128, 512], F32), ps("psSb", [128, 512], F32)]
        block = stack.enter_context(nc.Block())

        ctx = Ctx(nc, block, dummy=dmy)
        ld, mm, ev, ac, st = (ctx.sem(x) for x in ("ld", "mm", "ev", "ac", "st"))
        ctx.init_cpool(cpool, [1.0])

        @block.sync
        def _(sy):
            ld.dma(sy.dma_start(out=identb[:], in_=cst["c_ident_bf"][:, :]))
            ld.dma(sy.dma_start(out=u64b[:], in_=lcst["c_u64_bf"][:, :]))
            ld.dma(sy.dma_start(out=negu[:], in_=lcst["c_negu64_bf"][:, :]))
            ld.dma(sy.dma_start(out=lstr[:], in_=lcst["c_lstrict_bf"][:, :]))
            ld.dma(sy.dma_start(out=masku[:], in_=lcst["c_masku_f"][:, :]))
            ld.dma(sy.dma_start(out=onec[:], in_=lcst["c_onescol_bf"][:, :]))
            ld.dma(sy.dma_start(out=ones_bf[:], in_=cst["c_ones_bf_1x128"][:, :]))
            ld.dma(sy.dma_start(out=wcv[:], in_=wconv[:, :]))
            ld.dma(sy.dma_start(out=bcv[:], in_=bconv[:, :]))
            ld.dma(sy.dma_start(out=dtbc[:], in_=dtb[:, :]))
            ld.dma(sy.dma_start(out=alr[:], in_=alog[:, :]))
            ld.dma(sy.dma_start(out=dxp[:], in_=dexp_in[:, :]))
            ld.dma(sy.dma_start(out=dtr[:], in_=zx[8448:8512, :]))

        @block.vector
        def _(v):
            v.memset(Scur[0][:], 0.0)
            v.memset(pcol[:, 0:1], 0.0)
            v.memset(dmy[:], 0.0)
        ctx.barrier()

        cld = ctx.slot_sems("cld", 2)
        xst = ctx.slot_sems("xst", 2)
        for k in range(34):
            slx = k % 2
            ev_prev = ev.n

            @block.sync
            def _(sy, k=k, slx=slx, ev_prev=ev_prev):
                if k >= 2:
                    sy.wait_ge(ev.h, ev_prev - 1)
                sy.dma_start(out=xc[:, slx * 2056: slx * 2056 + 3],
                             in_=xprev[k * 128:(k + 1) * 128, :]).then_inc(cld[slx].h, 16)
                sy.dma_start(out=xc[:, slx * 2056 + 3: slx * 2056 + 2051],
                             in_=zx[4096 + k * 128: 4096 + (k + 1) * 128, :]).then_inc(cld[slx].h, 16)
                cld[slx].n += 32

            @block.vector
            def _(v, k=k, slx=slx):
                v.wait_ge(cld[slx].h, cld[slx].n)
                w = lambda j: xc[:, slx * 2056 + j: slx * 2056 + j + 2048]
                v.tensor_scalar(cta[:], w(0), wcv[:, k * 4: k * 4 + 1], None, ALU.mult)
                v.scalar_tensor_tensor(ctb[:], w(1), wcv[:, k * 4 + 1: k * 4 + 2], cta[:],
                                       ALU.mult, ALU.add)
                v.scalar_tensor_tensor(cta[:], w(2), wcv[:, k * 4 + 2: k * 4 + 3], ctb[:],
                                       ALU.mult, ALU.add)
                v.scalar_tensor_tensor(ctb[:], w(3), wcv[:, k * 4 + 3: k * 4 + 4], cta[:],
                                       ALU.mult, ALU.add)
                ev.inc(v.tensor_scalar(cta[:], ctb[:], bcv[:, k: k + 1], 1.0, ALU.add, ALU.mult))
            evc = ev.n

            @block.scalar
            def _(s, evc=evc):
                s.wait_ge(ev.h, evc)
                ac.inc(s.activation(ctc[:], cta[:], AF.Sigmoid))
            acs = ac.n

            @block.vector
            def _(v, k=k, slx=slx, acs=acs):
                v.wait_ge(ac.h, acs)
                if k < 32:
                    v.wait_ge(xst[slx].h, xst[slx].n)
                    ev.inc(v.tensor_tensor(xbcb[:, slx * 2048:(slx + 1) * 2048],
                                           cta[:], ctc[:], ALU.mult))
                elif k == 32:
                    ev.inc(v.tensor_tensor(Bfm[:], cta[:], ctc[:], ALU.mult))
                else:
                    ev.inc(v.tensor_tensor(Cfm[:], cta[:], ctc[:], ALU.mult))
            evg = ev.n

            @block.gpsimd
            def _(g, k=k, slx=slx, evg=evg):
                if k < 32:
                    g.wait_ge(ev.h, evg)
                    g.dma_start(out=x_fm[k * 128:(k + 1) * 128, :],
                                in_=xbcb[:, slx * 2048:(slx + 1) * 2048]).then_inc(xst[slx].h, 16)
                    xst[slx].n += 16
                elif k == 33:
                    g.wait_ge(ev.h, evg)
                    g.dma_start(out=C_st[:, :], in_=Cfm[:]).then_inc(st.h, 16)
                    st.n += 16

        @block.vector
        def _(v):
            v.tensor_scalar(d64a[:], dtr[:], dtbc[:, 0:1], 1.0, ALU.add, ALU.mult)
            v.tensor_scalar(d64b[:], d64a[:], -1.0, None, ALU.mult)
            ev.inc(v.tensor_max(d64b[:], d64a[:], d64b[:]))
        ev1 = ev.n

        @block.scalar
        def _(s, ev1=ev1):
            s.wait_ge(ev.h, ev1)
            s.activation(dtr[:], d64b[:], AF.Exp, scale=-1.0)
            ac.inc(s.activation(d64b[:], dtr[:], AF.Ln, bias=1.0))
        ac1 = ac.n

        @block.vector
        def _(v, ac1=ac1):
            v.wait_ge(ac.h, ac1)
            v.tensor_scalar_max(dtr[:], d64a[:], 0.0)
            ev.inc(v.tensor_tensor(d64a[:], dtr[:], d64b[:], ALU.add))
        ev2 = ev.n

        @block.scalar
        def _(s, ev2=ev2):
            s.wait_ge(ev.h, ev2)
            ac.inc(s.copy(dtspb[:], d64a[:]))
            s.activation(erow[:], alr[:], AF.Exp)
            ac.inc(s.activation(negrow[:], erow[:], AF.Copy, scale=-1.0))
        ac2 = ac.n

        @block.tensor
        def _(t, ac2=ac2):
            t.wait_ge(ac.h, ac2)
            mm.inc(t.matmul(psRow[0:64, 0:64], lhsT=ones_bf[0:1, 0:64], rhs=negrow[:],
                            start=True, stop=True, skip_group_check=True))
        mm3 = mm.n

        @block.scalar
        def _(s, mm3=mm3):
            s.wait_ge(mm.h, mm3)
            ac.inc(s.copy(Arep[:], psRow[0:64, 0:64]))
        ctx.barrier()

        xld = ctx.slot_sems("xld", 2)
        ysl = ctx.slot_sems("ysl", 2)
        dsl = ctx.slot_sems("dsl", 2)
        x3 = x_fm.rearrange("(k p) t -> p k t", p=128)
        y3 = y_fm.rearrange("(k p) t -> p k t", p=128)
        ev_yback = {}
        mm_yback = {}
        ev_grp_end = {}
        mm_grp_end = {}
        ac_cast = {}
        ev_pcol = {}
        ac_rowevac = {}
        for c in range(L0B_NCK):
            sl = c % 2
            t0 = c * L0B_CL

            @block.sync
            def _(sy, c=c, sl=sl, t0=t0, _evyb=ev_yback.get(c - 2, 0)):
                if c >= 2:
                    sy.wait_ge(ev.h, _evyb)
                sy.dma_start(out=xstage[:, sl * 2048:(sl + 1) * 2048]
                             .rearrange("p (k w) -> p k w", k=32),
                             in_=x3[:, :, t0:t0 + L0B_CL]).then_inc(xld[sl].h, 16)
                xld[sl].n += 16

            @block.tensor
            def _(t, c=c, sl=sl, t0=t0, _evyb=ev_yback.get(c - 1, 0)):
                if c >= 1:
                    t.wait_ge(ev.h, _evyb)
                mm.inc(t.transpose(psT[0:64, 0:64], dtspb[:, t0:t0 + 64], identb[0:64, 0:64]))
                mm.inc(t.transpose(psT[0:64, 64:192], Bfm[:, t0:t0 + 64], identb[:, :]))
            mmtb = mm.n

            @block.scalar
            def _(s, c=c, sl=sl, mmtb=mmtb, _evg=ev_grp_end.get(c - 2, 0)):
                s.wait_ge(mm.h, mmtb)
                if c >= 2:
                    s.wait_ge(ev.h, _evg)
                ac.inc(s.copy(dtT[:, sl * 64:(sl + 1) * 64], psT[0:64, 0:64]))
                ac.inc(s.copy(BT[:, sl * 128:(sl + 1) * 128], psT[0:64, 64:192]))
            actb = ac.n

            @block.vector
            def _(v, sl=sl, actb=actb):
                v.wait_ge(ac.h, actb)
                ev.inc(v.tensor_tensor(aT[:, sl * 64:(sl + 1) * 64],
                                       dtT[:, sl * 64:(sl + 1) * 64], Arep[:], ALU.mult))
            evaT = ev.n

            ev_xb = 0
            for b in range(4):
                @block.tensor
                def _(t, c=c, sl=sl, b=b, actb=actb, _evxb=ev_xb):
                    if b == 0:
                        t.wait_ge(ac.h, actb)
                        xld[sl].wait(t)
                    else:
                        t.wait_ge(ev.h, _evxb)
                    for j in range(8):
                        kk = b * 8 + j
                        mm.inc(t.transpose(psT[0:64, j * 128:(j + 1) * 128],
                                           xstage[:, sl * 2048 + kk * 64: sl * 2048 + (kk + 1) * 64],
                                           identb[:, :]))
                mmxb = mm.n

                @block.vector
                def _(v, c=c, sl=sl, b=b, mmxb=mmxb, _mmg=mm_grp_end.get(c - 2, 0)):
                    v.wait_ge(mm.h, mmxb)
                    if b == 0 and c >= 2:
                        v.wait_ge(mm.h, _mmg)
                    ev.inc(v.tensor_tensor(
                        xtl[:, sl * 4096 + b * 1024: sl * 4096 + (b + 1) * 1024]
                        .rearrange("p (g w) -> p g w", g=16),
                        psT[0:64, 0:1024].rearrange("p (g w) -> p g w", g=16),
                        dtT[:, sl * 64 + b * 16: sl * 64 + (b + 1) * 16]
                        .rearrange("p (g o) -> p g o", o=1).broadcast_to((64, 16, 64)),
                        ALU.mult))
                ev_xb = ev.n
            ev_xtl = ev.n

            @block.tensor
            def _(t, c=c, sl=sl, t0=t0, evaT=evaT,
                  _ace=ac_rowevac.get(c - 1, 0), _evp=ev_pcol.get(c - 1, 0)):
                t.wait_ge(ev.h, evaT)
                if c >= 1:
                    t.wait_ge(ac.h, _ace)
                    t.wait_ge(ev.h, _evp)
                s_aT = aT[:, sl * 64:(sl + 1) * 64]
                mm.inc(t.matmul(psRow[0:64, 0:64], lhsT=u64b[:], rhs=s_aT,
                                start=True, stop=True, skip_group_check=True))
                mm.inc(t.matmul(psRow[0:64, 128:192], lhsT=s_aT, rhs=u64b[:],
                                start=True, stop=True, skip_group_check=True))
                mm.inc(t.matmul(psRow[0:64, 192:256], lhsT=Bfm[:, t0:t0 + 64],
                                rhs=Cfm[:, t0:t0 + 64],
                                start=True, stop=True, skip_group_check=True))
                mm.inc(t.matmul(psRow[0:1, 320:384], lhsT=onec[0:64, 0:1], rhs=s_aT,
                                start=True, stop=True, skip_group_check=True))
                mm.inc(t.matmul(psRow[0:64, 384:385], lhsT=s_aT, rhs=onec[0:64, 0:1],
                                start=True, stop=True, skip_group_check=True))
            mm_s1 = mm.n

            @block.scalar
            def _(s, sl=sl, mm_s1=mm_s1):
                s.wait_ge(mm.h, mm_s1)
                ac.inc(s.copy(asrb[:, sl * 64:(sl + 1) * 64], psRow[0:1, 320:384]))
                ac.inc(s.activation(easr[:, sl * 64:(sl + 1) * 64], psRow[0:1, 320:384], AF.Exp))
            ac_s1 = ac.n

            @block.tensor
            def _(t, sl=sl, ac_s1=ac_s1):
                t.wait_ge(ac.h, ac_s1)
                mm.inc(t.matmul(psRow[0:64, 64:128], lhsT=negu[:], rhs=aT[:, sl * 64:(sl + 1) * 64],
                                start=True, stop=False, skip_group_check=True))
                mm.inc(t.matmul(psRow[0:64, 64:128], lhsT=ones_bf[0:1, 0:64],
                                rhs=asrb[:, sl * 64:(sl + 1) * 64],
                                start=False, stop=True, skip_group_check=True))
                mm.inc(t.matmul(psRow[:, 256:320], lhsT=ones_bf[:],
                                rhs=easr[:, sl * 64:(sl + 1) * 64],
                                start=True, stop=True, skip_group_check=True))
            mm_s2 = mm.n

            @block.scalar
            def _(s, c=c, sl=sl, mm_s2=mm_s2, _evsc=ev_grp_end.get(c - 1, 0),
                  _mmg=mm_grp_end.get(c - 2, 0)):
                s.wait_ge(mm.h, mm_s2)
                s.wait_ge(dsl[sl].h, dsl[sl].n)
                if c >= 1:
                    s.wait_ge(ev.h, _evsc)
                if c >= 2:
                    s.wait_ge(mm.h, _mmg)
                ac.inc(s.activation(sdoT[:, sl * 64:(sl + 1) * 64], psRow[0:64, 0:64], AF.Exp))
                ac.inc(s.activation(dsT[:, sl * 64:(sl + 1) * 64], psRow[0:64, 64:128], AF.Exp))
                ac.inc(s.activation(dcs[:, sl * 64:(sl + 1) * 64], psRow[0:64, 128:192], AF.Exp,
                                    bias=pcol[:, c % 2: c % 2 + 1]))
                ac.inc(s.copy(Emat[:, sl * 64:(sl + 1) * 64], psRow[:, 256:320]))
                ac.inc(s.copy(ascol[:, sl: sl + 1], psRow[0:64, 384:385]))
                ac.inc(s.copy(Sbf[:, sl * INNER:(sl + 1) * INNER], Scur[sl][:]))
            ac_rowevac[c] = ac.n
            ac_cast[c] = ac.n

            @block.vector
            def _(v, c=c, sl=sl, _ac=ac_rowevac[c]):
                v.wait_ge(ac.h, _ac)
                ev.inc(v.tensor_tensor(Cm[:, sl * 64:(sl + 1) * 64], psRow[0:64, 192:256],
                                       masku[:], ALU.mult))
                ev.inc(v.tensor_tensor(pcol[:, (c + 1) % 2:(c + 1) % 2 + 1],
                                       pcol[:, c % 2: c % 2 + 1], ascol[:, sl: sl + 1], ALU.add))
            ev_pcol[c] = ev.n

            @block.gpsimd
            def _(g, c=c, sl=sl, t0=t0, _ac=ac_rowevac[c]):
                g.wait_ge(ac.h, _ac)
                g.dma_start(out=dcum[:, t0:t0 + 64], in_=dcs[:, sl * 64:(sl + 1) * 64]
                            ).then_inc(dsl[sl].h, 16)
                dsl[sl].n += 16

            mmD = {}
            acE = {}
            evG = {}
            mmYd = {}
            mmSt = {}
            evYe = {}
            evSc = {}
            for g in range(16):
                gs = g % 2

                @block.vector
                def _(v, c=c, sl=sl, g=g, gs=gs, _mmD=mmD.get(g - 2, 0)):
                    if g >= 2:
                        v.wait_ge(mm.h, _mmD)
                    ev.inc(v.tensor_tensor(
                        Rb[:, gs * 256:(gs + 1) * 256].rearrange("p (h w) -> p h w", h=4),
                        u64b[:].rearrange("p (o w) -> p o w", o=1).broadcast_to((64, 4, 64)),
                        aT[:, sl * 64 + g * 4: sl * 64 + (g + 1) * 4]
                        .rearrange("p (h o) -> p h o", o=1).broadcast_to((64, 4, 64)),
                        ALU.mult))
                evR = ev.n

                @block.tensor
                def _(t, g=g, gs=gs, evR=evR, _acE=acE.get(g - 2, 0)):
                    t.wait_ge(ev.h, evR)
                    if g >= 2:
                        t.wait_ge(ac.h, _acE)
                    mm.inc(t.matmul(psD[gs][0:64, 0:256], lhsT=lstr[:],
                                    rhs=Rb[:, gs * 256:(gs + 1) * 256],
                                    start=True, stop=True, skip_group_check=True))
                mmD[g] = mm.n

                @block.scalar
                def _(s, g=g, gs=gs, _mmD=mmD[g], _evG=evG.get(g - 2, 0)):
                    s.wait_ge(mm.h, _mmD)
                    if g >= 2:
                        s.wait_ge(ev.h, _evG)
                    ac.inc(s.activation(Lsb[:, gs * 256:(gs + 1) * 256], psD[gs][0:64, 0:256],
                                        AF.Exp))
                acE[g] = ac.n

                @block.vector
                def _(v, c=c, sl=sl, g=g, gs=gs, _acE=acE[g],
                      _mmYd=mmYd.get(g - 2, 0), _mmSt=mmSt.get(g - 2, 0)):
                    v.wait_ge(ac.h, _acE)
                    if g >= 2:
                        v.wait_ge(mm.h, _mmYd)
                        v.wait_ge(mm.h, _mmSt)
                    ev.inc(v.tensor_tensor(
                        Gb[:, gs * 256:(gs + 1) * 256].rearrange("p (h w) -> p h w", h=4),
                        Lsb[:, gs * 256:(gs + 1) * 256].rearrange("p (h w) -> p h w", h=4),
                        Cm[:, sl * 64:(sl + 1) * 64].rearrange("p (o w) -> p o w", o=1)
                        .broadcast_to((64, 4, 64)), ALU.mult))
                    ev.inc(v.tensor_tensor(
                        Bt4[:, gs * 512:(gs + 1) * 512].rearrange("p (h w) -> p h w", h=4),
                        BT[:, sl * 128:(sl + 1) * 128].rearrange("p (o w) -> p o w", o=1)
                        .broadcast_to((64, 4, 128)),
                        dsT[:, sl * 64 + g * 4: sl * 64 + (g + 1) * 4]
                        .rearrange("p (h o) -> p h o", o=1).broadcast_to((64, 4, 128)),
                        ALU.mult))
                evG[g] = ev.n

                @block.tensor
                def _(t, c=c, sl=sl, g=g, gs=gs, t0=t0, _evG=evG[g], _ac=ac_rowevac[c],
                      _evxtl=ev_xtl, _evYe=evYe.get(g - 2, 0), _evSc=evSc.get(g - 2, 0)):
                    t.wait_ge(ev.h, _evG)
                    if g == 0:
                        t.wait_ge(ac.h, _ac)
                        t.wait_ge(ev.h, _evxtl)
                    if g >= 2:
                        t.wait_ge(ev.h, _evYe)
                        t.wait_ge(ev.h, _evSc)
                    for hh in range(4):
                        h = g * 4 + hh
                        mm.inc(t.matmul(psY[gs][0:64, hh * 64:(hh + 1) * 64],
                                        lhsT=Cfm[:, t0:t0 + 64],
                                        rhs=Sbf[:, sl * INNER + h * 64: sl * INNER + (h + 1) * 64],
                                        start=True, stop=True, skip_group_check=True))
                    for hh in range(4):
                        h = g * 4 + hh
                        mm.inc(t.matmul(psY[gs][0:64, 256 + hh * 64: 256 + (hh + 1) * 64],
                                        lhsT=Gb[:, gs * 256 + hh * 64: gs * 256 + (hh + 1) * 64],
                                        rhs=xtl[:, sl * 4096 + h * 64: sl * 4096 + (h + 1) * 64],
                                        start=True, stop=True, skip_group_check=True))
                mmYd[g] = mm.n

                @block.tensor
                def _(t, sl=sl, g=g, gs=gs):
                    for hh in range(4):
                        h = g * 4 + hh
                        mm.inc(t.matmul(psS[gs][0:128, hh * 64:(hh + 1) * 64],
                                        lhsT=Bt4[:, gs * 512 + hh * 128: gs * 512 + (hh + 1) * 128],
                                        rhs=xtl[:, sl * 4096 + h * 64: sl * 4096 + (h + 1) * 64],
                                        start=True, stop=True, skip_group_check=True))
                mmSt[g] = mm.n

                @block.vector
                def _(v, c=c, sl=sl, g=g, gs=gs, _mmYd=mmYd[g],
                      _mmyb=mm_yback.get(c - 2, 0)):
                    v.wait_ge(mm.h, _mmYd)
                    if g == 0 and c >= 2:
                        v.wait_ge(mm.h, _mmyb)
                    v.tensor_tensor(
                        tmpY[:].rearrange("p (h w) -> p h w", h=4),
                        psY[gs][0:64, 0:256].rearrange("p (h w) -> p h w", h=4),
                        sdoT[:, sl * 64 + g * 4: sl * 64 + (g + 1) * 4]
                        .rearrange("p (h o) -> p h o", o=1).broadcast_to((64, 4, 64)),
                        ALU.mult)
                    ev.inc(v.tensor_tensor(yT[:, sl * 4096 + g * 256: sl * 4096 + (g + 1) * 256],
                                           tmpY[:], psY[gs][0:64, 256:512], ALU.add))
                evYe[g] = ev.n

                @block.vector
                def _(v, c=c, sl=sl, g=g, gs=gs, _mmSt=mmSt[g], _accast=ac_cast.get(c - 1, 0)):
                    v.wait_ge(mm.h, _mmSt)
                    if g == 0 and c >= 1:
                        v.wait_ge(ac.h, _accast)
                    v.tensor_tensor(
                        tmpS[:].rearrange("p (h w) -> p h w", h=4),
                        Scur[sl][:, g * 256:(g + 1) * 256].rearrange("p (h w) -> p h w", h=4),
                        Emat[:, sl * 64 + g * 4: sl * 64 + (g + 1) * 4]
                        .rearrange("p (h o) -> p h o", o=1).broadcast_to((128, 4, 64)),
                        ALU.mult)
                    ev.inc(v.tensor_tensor(Scur[1 - sl][:, g * 256:(g + 1) * 256],
                                           tmpS[:], psS[gs][0:128, 0:256], ALU.add))
                evSc[g] = ev.n
            ev_grp_end[c] = ev.n
            mm_grp_end[c] = mm.n

            ev_stt = 0
            for b in range(4):
                @block.tensor
                def _(t, c=c, sl=sl, b=b, _evYl=evYe[15], _evstt=ev_stt):
                    if b == 0:
                        t.wait_ge(ev.h, _evYl)
                    else:
                        t.wait_ge(ac.h, ac.n)
                    for j in range(8):
                        kk = b * 8 + j
                        mm.inc(t.transpose(psT[:, j * 64:(j + 1) * 64],
                                           yT[:, sl * 4096 + kk * 128: sl * 4096 + (kk + 1) * 128],
                                           identb[0:64, 0:64]))
                mmyb = mm.n

                @block.scalar
                def _(s, b=b, mmyb=mmyb, _evstt=ev_stt):
                    s.wait_ge(mm.h, mmyb)
                    if b >= 1:
                        s.wait_ge(ev.h, _evstt)
                    ac.inc(s.copy(ytmp[:], psT[0:128, 0:512]))
                acyb = ac.n

                @block.vector
                def _(v, c=c, sl=sl, b=b, acyb=acyb, _ysl=ysl[sl].n):
                    v.wait_ge(ac.h, acyb)
                    if b == 0:
                        v.wait_ge(ysl[sl].h, _ysl)
                    for j in range(8):
                        kk = b * 8 + j
                        ev.inc(v.scalar_tensor_tensor(
                            ystage[:, sl * 2048 + kk * 64: sl * 2048 + (kk + 1) * 64],
                            xstage[:, sl * 2048 + kk * 64: sl * 2048 + (kk + 1) * 64],
                            dxp[:, kk: kk + 1], ytmp[:, j * 64:(j + 1) * 64],
                            ALU.mult, ALU.add))
                ev_stt = ev.n
            ev_yback[c] = ev.n
            mm_yback[c] = mm.n

            @block.gpsimd
            def _(g, c=c, sl=sl, t0=t0, _evst=ev_yback[c]):
                g.wait_ge(ev.h, _evst)
                g.dma_start(out=y3[:, :, t0:t0 + L0B_CL],
                            in_=ystage[:, sl * 2048:(sl + 1) * 2048]
                            .rearrange("p (k w) -> p k w", k=32)).then_inc(ysl[sl].h, 16)
                ysl[sl].n += 16

        @block.sync
        def _(sy):
            sy.wait_ge(ev.h, ev_grp_end[L0B_NCK - 1])
            sy.wait_ge(ev.h, ev_pcol[L0B_NCK - 1])
            st.dma(sy.dma_start(out=S_loc[:, :], in_=Scur[L0B_NCK % 2][:]))
            st.dma(sy.dma_start(out=atot[:, :], in_=pcol[:, L0B_NCK % 2: L0B_NCK % 2 + 1]))

        @block.sync
        def _(sy):
            st.wait(sy)
            for sem in ysl:
                sem.wait(sy)
            for sem in dsl:
                sem.wait(sy)
    return nc


def build_l0b_fix():
    nc = bass.Bass("TRN2", target_bir_lowering=False)
    res = nc.dram_tensor("res", [D, T], F32, kind="ExternalInput").ap()
    zx = nc.dram_tensor("zx", [L0B_DPAD, T], F32, kind="ExternalInput").ap()
    y_in = nc.dram_tensor("y_in", [INNER, T], BF16, kind="ExternalInput").ap()
    dcum_in = nc.dram_tensor("dcum_in", [64, T], F32, kind="ExternalInput").ap()
    C_in = nc.dram_tensor("C_in", [128, T], BF16, kind="ExternalInput").ap()
    S_pack = nc.dram_tensor("S_pack", [384, INNER], F32, kind="ExternalInput").ap()
    apack = nc.dram_tensor("apack", [3, 64], F32, kind="ExternalInput").ap()
    wot = nc.dram_tensor("wot", [INNER, D], F32, kind="ExternalInput").ap()
    nw_in = nc.dram_tensor("nw", [128, 32], F32, kind="ExternalInput").ap()
    res_out = nc.dram_tensor("res_out", [D, T], F32, kind="ExternalOutput").ap()
    ygd = nc.dram_tensor("ygd", [INNER, T], BF16, kind="Internal").ap()
    cst = declare_consts(nc)
    lcst = l0b_declare(nc)

    with ExitStack() as stack:
        sb = lambda *a, **k: stack.enter_context(nc.sbuf_tensor(*a, **k))
        ps = lambda *a, **k: stack.enter_context(nc.psum_tensor(*a, **k))
        Csb = sb("Csb", [128, T], BF16)
        Spk = sb("Spk", [128, 2 * 1024], F32)
        Sini = sb("Sini", [128, INNER], F32)
        Sib = sb("Sib", [128, INNER], BF16)
        stmp = sb("stmp", [128, 1024], F32)
        aprow = sb("aprow", [1, 3 * 64], F32)
        frow = sb("frow", [1, 2 * 64], BF16)
        Fmat = sb("Fmat", [128, 2 * 64], F32)
        d2 = sb("d2", [2, 4 * 512], BF16)
        dcrep = sb("dcrep", [128, 2 * 512], F32)
        ytl = sb("ytl", [128, 2 * 512], BF16)
        wtl2 = None
        ycr = sb("ycr", [128, 2 * 512], F32)
        y2t = sb("y2t", [128, 2 * 512], BF16)
        zt = sb("zt", [128, 2048], F32)
        zs = sb("zs", [128, 2048], F32)
        ybig = sb("ybig", [128, 2048], BF16)
        ybf = sb("ybf", [128, 2048], F32)
        ygt = sb("ygt", [128, 2048], F32)
        ygb = sb("ygb", [128, 2 * 2048], BF16)
        sqt = sb("sqt", [128, 2048], F32)
        rows = sb("rows", [1, 2 * T], F32)
        rows_b = sb("rows_b", [1, T], BF16)
        nwsb = sb("nwsb", [128, 32], F32)
        yst = sb("yst", [128, 32 * 512], BF16)
        wtl = sb("wtl", [128, 2 * 4096], BF16)
        rtl = sb("rtl", [128, 2 * 512], F32)
        outt = sb("outt", [128, 2 * 512], F32)
        ones_bf = sb("ones_bf", [1, 128], BF16)
        ones_f = sb("ones_f", [128, 1], F32)
        e2sb = sb("e2sb", [2, 128], BF16)
        cpool = sb("cpool", [128, 4], F32)
        dmy = sb("dmy", [1, 4], F32)
        psC = ps("psC", [128, 2 * 512], F32)
        psDc = ps("psDc", [128, 2 * 512], F32)
        psR = [psC[0:1, 0:512], psC[0:1, 512:1024], psDc[0:1, 0:512], psDc[0:1, 512:1024]]
        psRep = psC[0:128, 0:512]
        psO = psDc[0:128, 0:1024]
        block = stack.enter_context(nc.Block())

        ctx = Ctx(nc, block, dummy=dmy)
        ld, mm, ev, ac, st = (ctx.sem(x) for x in ("ld", "mm", "ev", "ac", "st"))
        ctx.init_cpool(cpool, [1e-5])

        @block.sync
        def _(sy):
            ld.dma(sy.dma_start(out=ones_bf[:], in_=cst["c_ones_bf_1x128"][:, :]))
            ld.dma(sy.dma_start(out=ones_f[:], in_=cst["c_ones_f_128x1"][:, :]))
            ld.dma(sy.dma_start(out=e2sb[:], in_=lcst["c_e2_bf"][:, :]))
            ld.dma(sy.dma_start(out=nwsb[:], in_=nw_in[:, :]))
            for j in range(3):
                ld.dma(sy.dma_start(out=aprow[:, j * 64:(j + 1) * 64], in_=apack[j:j + 1, :]))
            for tb4 in range(4):
                ld.dma(sy.dma_start(out=Csb[:, tb4 * 512:(tb4 + 1) * 512],
                                    in_=C_in[:, tb4 * 512:(tb4 + 1) * 512]))
        ctx.barrier()

        @block.vector
        def _(v):
            ev.inc(v.tensor_tensor(aprow[:, 0:64], aprow[:, 64:128], aprow[:, 128:192], ALU.add))
        e1 = ev.n

        @block.scalar
        def _(s, e1=e1):
            s.wait_ge(ev.h, e1)
            ac.inc(s.activation(frow[:, 0:64], aprow[:, 0:64], AF.Exp))
            ac.inc(s.activation(frow[:, 64:128], aprow[:, 128:192], AF.Exp))
        a1 = ac.n

        @block.tensor
        def _(t, a1=a1):
            t.wait_ge(ac.h, a1)
            mm.inc(t.matmul(psRep[:, 0:64], lhsT=ones_bf[:], rhs=frow[:, 0:64],
                            start=True, stop=True, skip_group_check=True))
            mm.inc(t.matmul(psRep[:, 64:128], lhsT=ones_bf[:], rhs=frow[:, 64:128],
                            start=True, stop=True, skip_group_check=True))
        m1 = mm.n

        @block.scalar
        def _(s, m1=m1):
            s.wait_ge(mm.h, m1)
            ac.inc(s.copy(Fmat[:, 0:64], psRep[:, 0:64]))
            ac.inc(s.copy(Fmat[:, 64:128], psRep[:, 64:128]))
        a2 = ac.n

        spl = ctx.slot_sems("spl", 2)
        for q in range(4):
            for j in range(3):
                i = q * 3 + j
                slq = i % 2

                @block.sync
                def _(sy, q=q, j=j, slq=slq, _ev=ev.n):
                    if i >= 2:
                        sy.wait_ge(ev.h, _ev)
                    sy.dma_start(out=Spk[:, slq * 1024:(slq + 1) * 1024],
                                 in_=S_pack[j * 128:(j + 1) * 128, q * 1024:(q + 1) * 1024]
                                 ).then_inc(spl[slq].h, 16)
                    spl[slq].n += 16

                @block.vector
                def _(v, q=q, j=j, slq=slq, a2=a2, _n=spl[slq].n):
                    v.wait_ge(spl[slq].h, _n)
                    v.wait_ge(ac.h, a2)
                    dst = Sini[:, q * 1024:(q + 1) * 1024]
                    src = Spk[:, slq * 1024:(slq + 1) * 1024]
                    if j < 2:
                        tgt = dst if j == 0 else stmp[:]
                        v.tensor_tensor(
                            tgt.rearrange("p (h w) -> p h w", w=64),
                            src.rearrange("p (h w) -> p h w", w=64),
                            Fmat[:, j * 64 + q * 16: j * 64 + (q + 1) * 16]
                            .rearrange("p (h o) -> p h o", o=1).broadcast_to((128, 16, 64)),
                            ALU.mult)
                        if j == 1:
                            ev.inc(v.tensor_tensor(dst, dst, stmp[:], ALU.add))
                        else:
                            ev.inc(v.tensor_copy(dmy[:], dmy[:]))
                    else:
                        ev.inc(v.tensor_tensor(dst, dst, src, ALU.add))
        evS = ev.n

        @block.scalar
        def _(s, evS=evS):
            s.wait_ge(ev.h, evS)
            ac.inc(s.copy(Sib[:], Sini[:]))
        acSib = ac.n
        ctx.barrier()

        d2l = ctx.slot_sems("d2l", 4)
        ytll = ctx.slot_sems("ytll", 2)
        y2s = ctx.slot_sems("y2s", 2)
        for i in range(128):
            k, tb = divmod(i, 4)
            slq = i % 2

            @block.gpsimd
            def _(g, k=k, tb=tb, i=i, _mm=mm.n):
                if i >= 4:
                    g.wait_ge(mm.h, _mm - 2)
                g.dma_start(out=d2[:, (i % 4) * 512:(i % 4 + 1) * 512],
                            in_=dcum_in[2 * k: 2 * k + 2, tb * 512:(tb + 1) * 512]
                            ).then_inc(d2l[i % 4].h, 16)
                d2l[i % 4].n += 16

            @block.sync
            def _(sy, k=k, tb=tb, i=i, slq=slq, _ev=ev.n):
                if i >= 2:
                    sy.wait_ge(ev.h, _ev - 1)
                sy.dma_start(out=ytl[:, slq * 512:(slq + 1) * 512],
                             in_=y_in[k * 128:(k + 1) * 128, tb * 512:(tb + 1) * 512]
                             ).then_inc(ytll[slq].h, 16)
                ytll[slq].n += 16

            @block.tensor
            def _(t, k=k, tb=tb, i=i, slq=slq, acSib=acSib, _ev=ev.n,
                  _d2n=d2l[i % 4].n):
                if i == 0:
                    t.wait_ge(ac.h, acSib)
                t.wait_ge(d2l[i % 4].h, _d2n)
                if i >= 2:
                    t.wait_ge(ev.h, _ev)
                t.matmul(psC[:, slq * 512:(slq + 1) * 512],
                         lhsT=Sib[:, k * 128:(k + 1) * 128],
                         rhs=Csb[:, tb * 512:(tb + 1) * 512],
                         start=True, stop=True, skip_group_check=True)
                mm.inc(t.matmul(psDc[:, slq * 512:(slq + 1) * 512],
                                lhsT=e2sb[:], rhs=d2[:, (i % 4) * 512:(i % 4 + 1) * 512],
                                start=True, stop=True, skip_group_check=True))
            mmi = mm.n

            @block.scalar
            def _(s, slq=slq, mmi=mmi):
                s.wait_ge(mm.h, mmi)
                ac.inc(s.copy(dcrep[:, slq * 512:(slq + 1) * 512],
                              psDc[:, slq * 512:(slq + 1) * 512]))
            aci = ac.n

            @block.vector
            def _(v, i=i, slq=slq, aci=aci, _ytn=ytll[slq].n, _y2n=y2s[slq].n):
                v.wait_ge(ac.h, aci)
                v.wait_ge(ytll[slq].h, _ytn)
                if i >= 2:
                    v.wait_ge(y2s[slq].h, _y2n)
                v.tensor_tensor(ycr[:, slq * 512:(slq + 1) * 512],
                                psC[:, slq * 512:(slq + 1) * 512],
                                dcrep[:, slq * 512:(slq + 1) * 512], ALU.mult)
                ev.inc(v.tensor_tensor(y2t[:, slq * 512:(slq + 1) * 512],
                                       ycr[:, slq * 512:(slq + 1) * 512],
                                       ytl[:, slq * 512:(slq + 1) * 512], ALU.add))
            evi = ev.n

            @block.sync
            def _(sy, k=k, tb=tb, slq=slq, evi=evi):
                sy.wait_ge(ev.h, evi)
                sy.dma_start(out=ygd[k * 128:(k + 1) * 128, tb * 512:(tb + 1) * 512],
                             in_=y2t[:, slq * 512:(slq + 1) * 512]).then_inc(y2s[slq].h, 16)
            y2s[slq].n += 16
        ctx.barrier()

        zld = ctx.slot_sems("zld", 1)
        ygs = ctx.slot_sems("ygs", 2)
        mmst0 = mm.n
        for k in range(32):
            slk = k % 2

            @block.sync
            def _(sy, k=k, _ev=ev.n):
                if k >= 1:
                    sy.wait_ge(ev.h, _ev)
                for tb4 in range(4):
                    sy.dma_start(out=zt[:, tb4 * 512:(tb4 + 1) * 512],
                                 in_=zx[k * 128:(k + 1) * 128, tb4 * 512:(tb4 + 1) * 512]
                                 ).then_inc(zld[0].h, 16)
                sy.dma_start(out=ybig[:, :], in_=ygd[k * 128:(k + 1) * 128, :]
                             ).then_inc(zld[0].h, 16)
                zld[0].n += 5 * 16

            @block.scalar
            def _(s, k=k, _n=zld[0].n):
                s.wait_ge(zld[0].h, _n)
                ac.inc(s.activation(zs[:], zt[:], AF.Sigmoid))
                ac.inc(s.copy(ybf[:], ybig[:]))
            ack = ac.n

            @block.vector
            def _(v, k=k, slk=slk, ack=ack, _yn=ygs[slk].n):
                v.wait_ge(ac.h, ack)
                if k >= 2:
                    v.wait_ge(ygs[slk].h, _yn)
                v.tensor_tensor(zs[:], zt[:], zs[:], ALU.mult)
                ev.inc(v.tensor_tensor(ygt[:], ybf[:], zs[:], ALU.mult))
            evk = ev.n

            @block.scalar
            def _(s, k=k, slk=slk, evk=evk):
                s.wait_ge(ev.h, evk)
                ac.inc(s.copy(ygb[:, slk * 2048:(slk + 1) * 2048], ygt[:]))
                ac.inc(s.activation(sqt[:], ygt[:], AF.Square))
            ack2 = ac.n

            @block.tensor
            def _(t, k=k, ack2=ack2):
                t.wait_ge(ac.h, ack2)
                for tb4 in range(4):
                    mm.inc(t.matmul(psR[tb4], lhsT=ones_f[:],
                                    rhs=sqt[:, tb4 * 512:(tb4 + 1) * 512],
                                    start=(k == 0), stop=(k == 31), skip_group_check=True))
            mmk = mm.n

            @block.gpsimd
            def _(g, k=k, slk=slk, ack2=ack2):
                g.wait_ge(ac.h, ack2)
                g.dma_start(out=ygd[k * 128:(k + 1) * 128, :],
                            in_=ygb[:, slk * 2048:(slk + 1) * 2048]).then_inc(ygs[slk].h, 16)
                ygs[slk].n += 16
        mmstE = mm.n

        @block.vector
        def _(v, mmstE=mmstE):
            v.wait_ge(mm.h, mmstE)
            for tb4 in range(4):
                ev.inc(v.tensor_copy(rows[:, tb4 * 512:(tb4 + 1) * 512], psR[tb4]))
        evrows = ev.n

        @block.scalar
        def _(s, evrows=evrows):
            s.wait_ge(ev.h, evrows)
            s.activation(rows[:, T:2 * T], rows[:, 0:T], AF.Ln, scale=1.0 / INNER, bias=1e-5)
            s.activation(rows[:, T:2 * T], rows[:, T:2 * T], AF.Exp, scale=-0.5)
            ac.inc(s.copy(rows_b[:], rows[:, T:2 * T]))
        acrstd = ac.n
        ctx.barrier()

        wls = ctx.slot_sems("wls", 4)
        rls = ctx.slot_sems("rls", 2)
        ots = ctx.slot_sems("ots", 2)
        for tb in range(NTB):
            mmrep = mm.n

            @block.tensor
            def _(t, tb=tb, _ev=ev.n):
                if tb >= 1:
                    t.wait_ge(ev.h, _ev)
                mm.inc(t.matmul(psRep[:, 0:512], lhsT=ones_bf[:],
                                rhs=rows_b[0:1, tb * 512:(tb + 1) * 512],
                                start=True, stop=True, skip_group_check=True))

            ygl = ctx.slot_sems("ygl%d" % tb, 2)
            for k in range(32):
                @block.sync
                def _(sy, tb=tb, k=k, _ev=ev.n):
                    if k >= 2:
                        sy.wait_ge(ev.h, _ev)
                    sy.dma_start(out=ytl[:, (k % 2) * 512:(k % 2 + 1) * 512],
                                 in_=ygd[k * 128:(k + 1) * 128, tb * 512:(tb + 1) * 512]
                                 ).then_inc(ygl[k % 2].h, 16)
                    ygl[k % 2].n += 16

                @block.vector
                def _(v, tb=tb, k=k, mmrep=mmrep, _n=ygl[k % 2].n, _mm=mm.n):
                    v.wait_ge(ygl[k % 2].h, _n)
                    v.wait_ge(mm.h, mmrep + 1)
                    if tb >= 1 and k == 0:
                        v.wait_ge(mm.h, _mm)
                    ev.inc(v.scalar_tensor_tensor(yst[:, k * 512:(k + 1) * 512],
                                                  ytl[:, (k % 2) * 512:(k % 2 + 1) * 512],
                                                  nwsb[:, k: k + 1], psRep[:, 0:512],
                                                  ALU.mult, ALU.mult))
            evyst = ev.n

            mmE = {}
            for oc in range(NK):
                pb = oc % 2

                @block.gpsimd
                def _(g, tb=tb, oc=oc, _mmp=mmE.get(oc - 2, 0)):
                    if tb * NK + oc >= 2:
                        g.wait_ge(mm.h, _mmp if oc >= 2 else mm.n)
                    for k in range(32):
                        g.dma_start(out=wtl[:, (oc % 2) * 4096 + k * 128:
                                        (oc % 2) * 4096 + (k + 1) * 128],
                                    in_=wot[k * 128:(k + 1) * 128, oc * 128:(oc + 1) * 128]
                                    ).then_inc(wls[oc % 2].h, 16)
                    wls[oc % 2].n += 32 * 16

                @block.sync
                def _(sy, tb=tb, oc=oc, pb=pb, _ev=ev.n):
                    if tb * NK + oc >= 2:
                        sy.wait_ge(ev.h, _ev)
                    sy.dma_start(out=rtl[:, pb * 512:(pb + 1) * 512],
                                 in_=res[oc * 128:(oc + 1) * 128, tb * 512:(tb + 1) * 512]
                                 ).then_inc(rls[pb].h, 16)
                    rls[pb].n += 16

                @block.tensor
                def _(t, tb=tb, oc=oc, pb=pb, evyst=evyst, _ev=ev.n,
                      _wn=wls[oc % 2].n):
                    t.wait_ge(ev.h, evyst)
                    t.wait_ge(wls[oc % 2].h, _wn)
                    if tb * NK + oc >= 2:
                        t.wait_ge(ev.h, _ev)
                    for k in range(32):
                        mm.inc(t.matmul(psO[:, pb * 512:(pb + 1) * 512],
                                        lhsT=wtl[:, (oc % 2) * 4096 + k * 128:
                                                 (oc % 2) * 4096 + (k + 1) * 128],
                                        rhs=yst[:, k * 512:(k + 1) * 512],
                                        start=(k == 0), stop=(k == 31), skip_group_check=True))
                mmE[oc] = mm.n

                @block.vector
                def _(v, oc=oc, pb=pb, _mm=mmE[oc], _rn=rls[pb].n, _on=ots[pb].n):
                    v.wait_ge(mm.h, _mm)
                    v.wait_ge(rls[pb].h, _rn)
                    if _on > 0:
                        v.wait_ge(ots[pb].h, _on)
                    ev.inc(v.tensor_tensor(outt[:, pb * 512:(pb + 1) * 512],
                                           psO[:, pb * 512:(pb + 1) * 512],
                                           rtl[:, pb * 512:(pb + 1) * 512], ALU.add))
                evoc = ev.n

                @block.sync
                def _(sy, tb=tb, oc=oc, pb=pb, evoc=evoc):
                    sy.wait_ge(ev.h, evoc)
                    sy.dma_start(out=res_out[oc * 128:(oc + 1) * 128, tb * 512:(tb + 1) * 512],
                                 in_=outt[:, pb * 512:(pb + 1) * 512]).then_inc(ots[pb].h, 16)
                ots[pb].n += 16

        @block.sync
        def _(sy):
            for sem in ots:
                sem.wait(sy)
            for sem in ygs:
                sem.wait(sy)
    return nc


def run_l0attnb(shards, inputs):
    xhat = run_ln_xhat(shards)
    prep = l0b_host_prep(inputs)
    cst = consts_for_core()
    lcst = l0b_consts()

    nc1 = build_l0b_proj()
    r1 = run_bass_kernel_spmd(nc1, [{"xhat": xhat[c], "w_t": prep["w_t"]}
                                    for c in range(NCORES)], core_ids=list(range(NCORES)))
    zxs = [r1.results[c]["zx"] for c in range(NCORES)]

    nc2 = build_l0b_main()
    in2 = []
    for c in range(NCORES):
        xprev = (np.zeros((L0B_CONVCH, 3), np.float32) if c % 4 == 0
                 else np.ascontiguousarray(zxs[c - 1][4096:8448, T - 3:T]))
        m = {"zx": zxs[c], "xprev": xprev, "wconv": prep["wconv"], "bconv": prep["bconv"],
             "dtb": prep["dtb"], "alog": prep["alog"], "dexp": prep["dexp"]}
        m.update(cst)
        m.update(lcst)
        in2.append(m)
    r2 = run_bass_kernel_spmd(nc2, in2, core_ids=list(range(NCORES)))

    nc3 = build_l0b_fix()
    in3 = []
    for c in range(NCORES):
        b0 = (c // 4) * 4
        S_pack = np.zeros((384, INNER), np.float32)
        apack = np.zeros((3, 64), np.float32)
        for j in range(3):
            src = c - 3 + j
            if src >= b0:
                S_pack[j * 128:(j + 1) * 128] = r2.results[src]["S_loc"]
                apack[j] = r2.results[src]["atot"][:, 0]
        m = {"res": shards[c], "zx": zxs[c], "y_in": r2.results[c]["y_fm"],
             "dcum_in": r2.results[c]["dcum"], "C_in": r2.results[c]["C_st"],
             "S_pack": S_pack, "apack": apack, "wot": prep["wot"], "nw": prep["nw"]}
        m.update(cst)
        m.update(lcst)
        in3.append(m)
    r3 = run_bass_kernel_spmd(nc3, in3, core_ids=list(range(NCORES)))
    return [r3.results[c]["res_out"] for c in range(NCORES)]
